# version 08
# baseline (speedup 1.0000x reference)
import numpy as np
import concourse.bass as bass
import concourse.mybir as mybir
from concourse.bass_utils import run_bass_kernel_spmd
from contextlib import ExitStack

F32 = mybir.dt.float32
BF16 = mybir.dt.bfloat16
AF = mybir.ActivationFunctionType
ALU = mybir.AluOpType
AX = mybir.AxisListType

NCORE = 8
NTOK = 2048
D = 1024
NDT = 8
CH = 128
NCH = NTOK // CH
BIG = 1e9


class Prog:
    NDSLOT = 6

    def __init__(self, nc, stack):
        self.nc = nc
        self.st = stack
        self.cnt = {"v": 0, "s": 0, "t": 0, "g": 0}
        self.sem = {k: stack.enter_context(nc.semaphore(f"sem_{k}"))
                    for k in self.cnt}
        for i in range(self.NDSLOT):
            k = f"d{i}"
            self.sem[k] = stack.enter_context(nc.semaphore(f"sem_{k}"))
            self.cnt[k] = 0
        self.dslot = 0
        self.ccsems = []
        self.blk = None

    def dma_slot(self):
        k = f"d{3 + (self.dslot % 2)}"
        self.dslot += 1
        return k

    def v(self, v, fn):
        v.wait_ge(self.sem["v"], self.cnt["v"])
        fn().then_inc(self.sem["v"], 1)
        self.cnt["v"] += 1
        return self.cnt["v"]

    def s(self, s, fn):
        s.wait_ge(self.sem["s"], self.cnt["s"])
        fn().then_inc(self.sem["s"], 1)
        self.cnt["s"] += 1
        return self.cnt["s"]

    def g(self, g, fn):
        g.wait_ge(self.sem["g"], self.cnt["g"])
        fn().then_inc(self.sem["g"], 1)
        self.cnt["g"] += 1
        return self.cnt["g"]

    def t(self, fn):
        fn().then_inc(self.sem["t"], 1)
        self.cnt["t"] += 1
        return self.cnt["t"]

    def dgroup(self, q, slot):
        if self.cnt[slot] > 0:
            q.wait_ge(self.sem[slot], self.cnt[slot])

    def dma(self, q, slot, out, in_):
        q.dma_start(out=out, in_=in_).then_inc(self.sem[slot], 16)
        self.cnt[slot] += 16
        return (slot, self.cnt[slot])

    def wait(self, eng, key, tok=None):
        if tok is None:
            key, tok = key
        if tok > 0:
            eng.wait_ge(self.sem[key], tok)

    def on_v(self, fn):
        self.blk.vector(fn)

    def on_s(self, fn):
        self.blk.scalar(fn)

    def on_t(self, fn):
        self.blk.tensor(fn)

    def on_g(self, fn):
        self.blk.gpsimd(fn)

    def on_q(self, fn):
        self.blk.sync(fn)


def emit_ln(p, B, PS):
    X, XN = B["X"], B["XN"]
    sums = B["FT"][0:1, 0:NTOK]
    sumsq = B["ROWF"][:, 0:NTOK]
    mean = sums
    var = sumsq
    std = sumsq
    rr = sumsq
    RBC = B["PC"][:, 3072:5120]
    MBC = B["PC"][:, 5120:7168]
    SQ = B["PC"][:, 0:512]
    TMP = B["FT"][:, 0:NTOK]

    ps_s1 = PS["r1"]
    ps_s2 = PS["r2"]
    ps_bc = PS["a"]
    onesc = B["onesc"]
    onesr = B["onesr"]
    tok_x = B["tok_x"][0]

    cast_toks = []

    def c_cast(s):
        p.wait(s, tok_x)
        for d in range(NDT):
            cast_toks.append(p.s(s, lambda d=d: s.copy(
                XN[:, d * NTOK:(d + 1) * NTOK], X[:, d * NTOK:(d + 1) * NTOK])))
    p.on_s(c_cast)
    tok_cast = p.cnt["s"]

    for nch in range(4):
        lo = nch * 512

        def c_sum(t, lo=lo):
            def f():
                p.wait(t, "s", tok_cast)
                p.wait(t, B["tok_consts"][0])
                p.wait(t, "v", p.cnt["v"])
                last = None
                for d in range(NDT):
                    last = t.matmul(ps_s1[:, :], onesc,
                                    XN[:, d * NTOK + lo:d * NTOK + lo + 512],
                                    start=(d == 0), stop=(d == NDT - 1))
                return last
            p.t(f)
        p.on_t(c_sum)
        tok_tsum = p.cnt["t"]

        for d in range(NDT):
            def c_sq(v, d=d, lo=lo):
                p.wait(v, "t", p.cnt["t"])
                p.wait(v, tok_x)
                p.v(v, lambda: v.tensor_mul(
                    SQ, X[:, d * NTOK + lo:d * NTOK + lo + 512],
                    X[:, d * NTOK + lo:d * NTOK + lo + 512]))
            p.on_v(c_sq)
            tok_v = p.cnt["v"]

            def c_mmsq(t, d=d, tok_v=tok_v):
                def f():
                    p.wait(t, "v", tok_v)
                    return t.matmul(ps_s2[:, :], onesc, SQ,
                                    start=(d == 0), stop=(d == NDT - 1))
                p.t(f)
            p.on_t(c_mmsq)
        tok_tsq = p.cnt["t"]

        def c_evac(v, lo=lo, a=tok_tsum, b=tok_tsq):
            p.wait(v, "t", b)
            p.v(v, lambda: v.tensor_copy(sums[:, lo:lo + 512], ps_s1[:, :]))
            p.v(v, lambda: v.tensor_copy(sumsq[:, lo:lo + 512], ps_s2[:, :]))
        p.on_v(c_evac)

    msq = B["DENF"][0:1, 0:256]

    def c_rows(v):
        p.v(v, lambda: v.tensor_scalar_mul(mean, sums, 1.0 / D))
        for i in range(8):
            sl = slice(i * 256, (i + 1) * 256)
            p.v(v, lambda sl=sl: v.tensor_mul(msq, mean[:, sl], mean[:, sl]))
            p.v(v, lambda sl=sl: v.scalar_tensor_tensor(
                var[:, sl], sumsq[:, sl], 1.0 / D, msq, ALU.mult, ALU.subtract))
    p.on_v(c_rows)
    tok_var = p.cnt["v"]

    def c_std(s):
        p.wait(s, "v", tok_var)
        p.s(s, lambda: s.activation(std, var, AF.Sqrt, bias=B["EPS"]))
    p.on_s(c_std)
    tok_std = p.cnt["s"]

    def c_recip(v):
        p.wait(v, "s", tok_std)
        p.v(v, lambda: v.reciprocal(rr, std))
    p.on_v(c_recip)

    onesrf = B["onesrf"]
    for row, dst in ((rr, RBC), (mean, MBC)):
        for nch in range(4):
            lo = nch * 512

            def c_bc(t, row=row, lo=lo):
                def f():
                    p.wait(t, "v", p.cnt["v"])
                    return t.matmul(ps_bc[:, :], onesrf, row[:, lo:lo + 512],
                                    start=True, stop=True)
                p.t(f)
            p.on_t(c_bc)
            tok_t = p.cnt["t"]

            def c_bev(v, dst=dst, lo=lo, tok_t=tok_t):
                p.wait(v, "t", tok_t)
                p.v(v, lambda: v.tensor_copy(dst[:, lo:lo + 512], ps_bc[:, :]))
            p.on_v(c_bev)

    def c_mbc(v):
        p.v(v, lambda: v.tensor_mul(MBC, MBC, RBC))
    p.on_v(c_mbc)

    def c_apply(v):
        for d in range(NDT):
            sl = slice(d * NTOK, (d + 1) * NTOK)
            p.v(v, lambda sl=sl: v.tensor_mul(TMP, X[:, sl], RBC))
            p.v(v, lambda sl=sl: v.tensor_sub(XN[:, sl], TMP, MBC))
    p.on_v(c_apply)
    return p.cnt["v"]


def emit_top2(p, v, SC, IO, W1O, W2O, rt):
    r0, r1, m1, m2 = rt["r0"], rt["r1"], rt["m1"], rt["m2"]
    w0, w1 = rt["w0"], rt["w1"]
    p.v(v, lambda: v.reduce_max(r0, SC, AX.X))
    p.v(v, lambda: v.tensor_scalar(w0, SC, r0, None, ALU.is_equal))
    p.v(v, lambda: v.tensor_scalar(w1, w0, -BIG, BIG, ALU.mult, ALU.add))
    p.v(v, lambda: v.scalar_tensor_tensor(w0, IO, 1.0, w0, ALU.mult, ALU.mult))
    p.v(v, lambda: v.tensor_add(w1, w1, w0))
    p.v(v, lambda: v.tensor_reduce(r1, w1, AX.X, ALU.min))
    p.v(v, lambda: v.tensor_scalar(W1O, IO, r1, None, ALU.is_equal))
    p.v(v, lambda: v.tensor_mul(w0, W1O, SC))
    p.v(v, lambda: v.tensor_reduce(m1, w0, AX.X, ALU.add))
    p.v(v, lambda: v.scalar_tensor_tensor(w1, W1O, -BIG, SC, ALU.mult, ALU.add))
    p.v(v, lambda: v.reduce_max(r0, w1, AX.X))
    p.v(v, lambda: v.tensor_scalar(w0, w1, r0, None, ALU.is_equal))
    p.v(v, lambda: v.tensor_scalar(w1, w0, -BIG, BIG, ALU.mult, ALU.add))
    p.v(v, lambda: v.scalar_tensor_tensor(w0, IO, 1.0, w0, ALU.mult, ALU.mult))
    p.v(v, lambda: v.tensor_add(w1, w1, w0))
    p.v(v, lambda: v.tensor_reduce(r1, w1, AX.X, ALU.min))
    p.v(v, lambda: v.tensor_scalar(W2O, IO, r1, None, ALU.is_equal))
    p.v(v, lambda: v.tensor_mul(w0, W2O, SC))
    p.v(v, lambda: v.tensor_reduce(m2, w0, AX.X, ALU.add))
    return m1, m2


def emit_gate_scores(p, B, PS, gateT, tok_gate, xn_ready, SCEM):
    XN = B["XN"]
    ps_sc = PS["r8"]

    for nch in range(4):
        lo = nch * 512

        def c_mm(t, lo=lo):
            def f():
                p.wait(t, tok_gate)
                p.wait(t, "v", max(xn_ready, p.cnt["v"]))
                last = None
                for d in range(NDT):
                    last = t.matmul(ps_sc[:, :], gateT[:, d * 8:(d + 1) * 8],
                                    XN[:, d * NTOK + lo:d * NTOK + lo + 512],
                                    start=(d == 0), stop=(d == NDT - 1))
                return last
            p.t(f)
        p.on_t(c_mm)
        tok_t = p.cnt["t"]

        def c_ev(v, lo=lo, tok_t=tok_t):
            p.wait(v, "t", tok_t)
            p.v(v, lambda: v.tensor_copy(SCEM[:, lo:lo + 512], ps_sc[:, :]))
        p.on_v(c_ev)
    return p.cnt["v"]


def emit_l0_ffn(p, B, W, PS):
    X, XN = B["X"], B["XN"]
    PA, PB, PC = B["PA"], B["PB"], B["PC"]
    XT = B["XTRA"]
    RT = B["RT"]
    IO8 = B["iota8"]
    ident = B["ident"]
    onesr = B["onesr"]

    xn_ready = emit_ln(p, B, PS)

    gateT = B["WG"]

    tok_gate_box = [None]

    def c_gl(g):
        p.dgroup(g, "d2")
        tok_gate_box[0] = p.dma(g, "d2",
                                gateT[:, :].rearrange("p (kt e) -> p kt e", e=8),
                                W["l0gate"].rearrange("(kt p) e -> p kt e", p=128))
    p.on_g(c_gl)
    tok_gate = tok_gate_box[0]

    SCEM = B["SC_EM"]
    tok_scores = emit_gate_scores(p, B, PS, gateT, tok_gate, xn_ready, SCEM)

    ps_trf = PS["o4"][:, 0:128]

    SCtm = RT[:, 0:8]
    W1O = RT[:, 8:16]
    W2O = RT[:, 16:24]
    WV = RT[:, 24:32]
    rt = {"r0": RT[:, 32:33], "r1": RT[:, 33:34],
          "m1": RT[:, 34:35], "m2": RT[:, 35:36],
          "w0": RT[:, 40:48], "w1": RT[:, 48:56]}
    d12 = RT[:, 36:37]
    sgm = RT[:, 37:38]
    wvf = B["WVF"]
    identf = B["identf"]
    wr_bf = B["WRBF"]

    def c_scbf(v):
        p.v(v, lambda: v.memset(wvf, 0.0))
    p.on_v(c_scbf)
    tok_scbf = p.cnt["v"]

    for ch in range(NCH):
        lo = ch * CH

        def c_tr(t, lo=lo):
            def f():
                p.wait(t, "v", max(tok_scores, tok_scbf, p.cnt["v"]))
                p.wait(t, B["tok_consts"][0])
                return t.transpose(ps_trf[:, 0:8], SCEM[:, lo:lo + CH],
                                   identf[0:8, 0:8])
            p.t(f)
        p.on_t(c_tr)
        tok_t = p.cnt["t"]

        def c_route(v, tok_t=tok_t):
            p.wait(v, "t", tok_t)
            p.wait(v, B["tok_consts"][0])
            p.v(v, lambda: v.tensor_copy(SCtm, ps_trf[:, 0:8]))
            m1, m2 = emit_top2(p, v, SCtm, IO8, W1O, W2O, rt)
            p.v(v, lambda: v.tensor_sub(d12, m1, m2))
        p.on_v(c_route)
        tok_v = p.cnt["v"]

        def c_sig(s, tok_v=tok_v):
            p.wait(s, "v", tok_v)
            p.s(s, lambda: s.activation(sgm, d12, AF.Sigmoid))
        p.on_s(c_sig)
        tok_s = p.cnt["s"]

        def c_wv(v, tok_s=tok_s):
            p.wait(v, "s", tok_s)
            p.v(v, lambda: v.tensor_scalar(WV, W1O, sgm, None, ALU.mult))
            p.v(v, lambda: v.tensor_scalar(rt["w0"], W2O, sgm, None, ALU.mult))
            p.v(v, lambda: v.tensor_sub(rt["w0"], W2O, rt["w0"]))
            p.v(v, lambda: v.tensor_add(WV, WV, rt["w0"]))
            p.v(v, lambda: v.tensor_copy(wvf[:, 0:8], WV))
        p.on_v(c_wv)
        tok_v2 = p.cnt["v"]

        def c_trb(t, tok_v2=tok_v2):
            def f():
                p.wait(t, "v", max(tok_v2, p.cnt["v"]))
                return t.transpose(ps_trf[:, :], wvf, identf)
            p.t(f)
        p.on_t(c_trb)
        tok_t2 = p.cnt["t"]

        def c_wrow(v, lo=lo, tok_t2=tok_t2):
            p.wait(v, "t", tok_t2)
            p.v(v, lambda: v.tensor_copy(wr_bf[:, lo:lo + CH], ps_trf[0:8, 0:CH]))
        p.on_v(c_wrow)
    tok_wrbf = p.cnt["v"]

    acc = PB
    wbc = PC[:, 0:NTOK]
    w1s = PA[:, 0:4096]
    w3s = PA[:, 4096:8192]
    w2s = PA[:, 8192:12288]
    hqs = [PC[:, 3072:5120], XT[:, 0:2048]]
    PAIRMODE = 1
    if PAIRMODE:
        pairs = [(PS["a"], PS["b"], PC[:, 2048:2560], PC[:, 2560:3072]),
                 (PS["c"], PS["d"], PC[:, 5120:5632], PC[:, 5632:6144])]
    else:
        pairs = [(PS["a"], PS["b"], PC[:, 2048:2560], PC[:, 2560:3072])] * 2
    ps_o = PS["o4"]
    ps_bcb = PS["bc"]

    fidx = [0]
    cidx = [0]
    h_tok = {0: 0, 1: 0}
    tok_p1_prev = [p.cnt["t"]]
    tok_p2_par = {0: p.cnt["t"], 1: p.cnt["t"]}
    tok_p2_prev = [p.cnt["t"]]
    acc_tok = [p.cnt["v"]]
    ld13 = [None]
    ld2t = [None]

    first = [True]
    for e in range(8):
        for nch in range(4):
            lo = nch * 512

            def c_bc(t, e=e, lo=lo):
                def f():
                    p.wait(t, "v", max(tok_wrbf, p.cnt["v"]))
                    p.wait(t, B["tok_consts"][0])
                    return t.matmul(ps_bcb[:, :], B["ESEL"][:, e * 128:(e + 1) * 128],
                                    wr_bf[:, lo:lo + 512],
                                    start=True, stop=True)
                p.t(f)
            p.on_t(c_bc)
            tok_t = p.cnt["t"]

            def c_bev(v, lo=lo, tok_t=tok_t):
                p.wait(v, "t", tok_t)
                p.v(v, lambda: v.tensor_copy(wbc[:, lo:lo + 512], ps_bcb[:, :]))
            p.on_v(c_bev)
        tok_wbc = p.cnt["v"]

        for q in range(4):
            def c_load13(g, e=e, q=q, pw=tok_p1_prev[0]):
                p.wait(g, "t", pw)
                slot = p.dma_slot()
                p.dgroup(g, slot)
                p.dma(g, slot, w1s.rearrange("p (kt f) -> p kt f", f=512),
                      W["l0w1"][e, :, q * 512:(q + 1) * 512]
                      .rearrange("(kt p) f -> p kt f", p=128))
                ld13[0] = p.dma(g, slot, w3s.rearrange("p (kt f) -> p kt f", f=512),
                                W["l0w3"][e, :, q * 512:(q + 1) * 512]
                                .rearrange("(kt p) f -> p kt f", p=128))
            p.on_g(c_load13)

            def c_load2(g, e=e, q=q, pw=tok_p2_prev[0]):
                p.wait(g, "t", pw)
                slot = p.dma_slot()
                p.dgroup(g, slot)
                ld2t[0] = p.dma(g, slot, w2s.rearrange("p (ft d) -> p ft d", d=1024),
                                W["l0w2"][e, q * 512:(q + 1) * 512, :]
                                .rearrange("(ft p) d -> p ft d", p=128))
            p.on_g(c_load2)

            pending = [None]

            def emit_phase2(info):
                if info is None:
                    return
                lo, hq, hlast = info
                for dh in range(2):
                    def c_o(t, dh=dh, h=hlast, lw=ld2t[0], aw=acc_tok[0], hq=hq):
                        def f():
                            p.wait(t, lw)
                            p.wait(t, "v", max(h, aw))
                            last = None
                            for f4 in range(4):
                                for dt in range(4):
                                    last = t.matmul(
                                        ps_o[:, dt * 512:(dt + 1) * 512],
                                        w2s[:, f4 * 1024 + dh * 512 + dt * 128:
                                            f4 * 1024 + dh * 512 + (dt + 1) * 128],
                                        hq[:, f4 * 512:(f4 + 1) * 512],
                                        start=(f4 == 0), stop=(f4 == 3))
                            return last
                        p.t(f)
                    p.on_t(c_o)
                    tok_o = p.cnt["t"]

                    def c_acc(v, dh=dh, lo=lo, tok_o=tok_o, fw=first[0]):
                        p.wait(v, "t", tok_o)
                        for dt in range(4):
                            d = dh * 4 + dt
                            dst = acc[:, d * NTOK + lo:d * NTOK + lo + 512]
                            src = ps_o[:, dt * 512:(dt + 1) * 512]
                            if fw:
                                p.v(v, lambda dst=dst, src=src: v.tensor_copy(dst, src))
                            else:
                                p.v(v, lambda dst=dst, src=src: v.tensor_add(dst, dst, src))
                    p.on_v(c_acc)
                    acc_tok[0] = p.cnt["v"]
                cp = (cidx[0] - 1) % 2
                tok_p2_prev[0] = p.cnt["t"]

            for c in range(4):
                lo = c * 512
                cpar = cidx[0] % 2
                cidx[0] += 1
                hq = hqs[cpar]
                for f in range(4):
                    par = fidx[0] % 2
                    fidx[0] += 1
                    pa, pb, t1, t2 = pairs[par]
                    hw_val = h_tok[par] if PAIRMODE else max(h_tok.values())
                    hw_val = max(hw_val, tok_wbc)

                    def c_ab(t, q=q, f=f, lo=lo, pa=pa, pb=pb,
                             hw=hw_val, lw=ld13[0]):
                        def fa():
                            p.wait(t, lw)
                            p.wait(t, "v", max(xn_ready, hw))
                            last = None
                            for d in range(NDT):
                                last = t.matmul(
                                    pa[:, :],
                                    w1s[:, d * 512 + f * 128:d * 512 + (f + 1) * 128],
                                    XN[:, d * NTOK + lo:d * NTOK + lo + 512],
                                    start=(d == 0), stop=(d == NDT - 1))
                            for d in range(NDT):
                                last = t.matmul(
                                    pb[:, :],
                                    w3s[:, d * 512 + f * 128:d * 512 + (f + 1) * 128],
                                    XN[:, d * NTOK + lo:d * NTOK + lo + 512],
                                    start=(d == 0), stop=(d == NDT - 1))
                            return last
                        p.t(fa)
                    p.on_t(c_ab)
                    tok_ab = p.cnt["t"]

                    def c_silu(sc, pa=pa, t1=t1, tok_ab=tok_ab, hw=hw_val):
                        p.wait(sc, "t", tok_ab)
                        p.wait(sc, "v", hw)
                        p.s(sc, lambda: sc.activation(t1, pa[:, :], AF.Silu))
                    p.on_s(c_silu)
                    tok_silu = p.cnt["s"]

                    def c_h(v, f=f, lo=lo, pb=pb, t1=t1, t2=t2, hq=hq,
                            tok_silu=tok_silu, p2w=tok_p2_par[cpar]):
                        p.wait(v, "s", tok_silu)
                        p.wait(v, "v", tok_wbc)
                        p.wait(v, "t", p2w)
                        p.v(v, lambda: v.tensor_mul(t2, pb[:, :], wbc[:, lo:lo + 512]))
                        p.v(v, lambda: v.tensor_mul(
                            hq[:, f * 512:(f + 1) * 512], t1, t2))
                    p.on_v(c_h)
                    h_tok[par] = p.cnt["v"]
                tok_p1_prev[0] = p.cnt["t"]
                hlast = p.cnt["v"]

                emit_phase2(pending[0])
                tok_p2_par[cpar ^ 1] = p.cnt["t"]
                pending[0] = (lo, hq, hlast)
            emit_phase2(pending[0])
            tok_p2_par[(cidx[0] - 1) % 2] = p.cnt["t"]
            if e == 0 and q == 0:
                first[0] = False

    def c_res(v):
        for d in range(NDT):
            sl = slice(d * NTOK, (d + 1) * NTOK)
            p.v(v, lambda sl=sl: v.scalar_tensor_tensor(
                X[:, sl], acc[:, sl], 0.5, X[:, sl], ALU.mult, ALU.add))
    p.on_v(c_res)




def emit_l1_ffn(p, B, W, PS):
    X, XN = B["X"], B["XN"]
    PA, PB, PC = B["PA"], B["PB"], B["PC"]
    RT = B["RT"]
    IO8 = B["iota8"]
    identf = B["identf"]
    ps_a, ps_b = PS["a"], PS["b"]
    ps_o = PS["o4"]

    xn_ready = emit_ln(p, B, PS)

    ups = PA[:, 0:4096]
    dns = PA[:, 4096:6144]
    h_all = PB
    t1 = PC[:, 2048:2560]

    for c in range(4):
        lo = c * 512
        for u8 in range(8):
            tok_w = [None]

            def c_ld(g, u8=u8):
                p.wait(g, "t", p.cnt["t"])
                slot = p.dma_slot()
                p.dgroup(g, slot)
                tok_w[0] = p.dma(
                    g, slot,
                    ups.rearrange("p (kt f) -> p kt f", f=512),
                    W["l1up"][:, u8 * 512:(u8 + 1) * 512]
                    .rearrange("(kt p) f -> p kt f", p=128))
            p.on_g(c_ld)

            for ut in range(4):
                def c_u(t, u8=u8, ut=ut, lo=lo, tw=tok_w[0]):
                    def f():
                        p.wait(t, tw)
                        p.wait(t, "v", max(xn_ready, p.cnt["v"]))
                        p.wait(t, "s", p.cnt["s"])
                        last = None
                        for d in range(NDT):
                            last = t.matmul(
                                ps_a[:, :],
                                ups[:, d * 512 + ut * 128:d * 512 + (ut + 1) * 128],
                                XN[:, d * NTOK + lo:d * NTOK + lo + 512],
                                start=(d == 0), stop=(d == NDT - 1))
                        return last
                    p.t(f)
                p.on_t(c_u)
                tok_u = p.cnt["t"]

                def c_r(s, tok_u=tok_u):
                    p.wait(s, "t", tok_u)
                    p.wait(s, "v", p.cnt["v"])
                    p.s(s, lambda: s.activation(t1, ps_a[:, :], AF.Relu))
                p.on_s(c_r)
                tok_r = p.cnt["s"]

                def c_sqh(v, u8=u8, ut=ut, tok_r=tok_r):
                    p.wait(v, "s", tok_r)
                    dst = h_all[:, (u8 * 4 + ut) * 512:(u8 * 4 + ut + 1) * 512]
                    p.v(v, lambda: v.tensor_mul(dst, t1, t1))
                p.on_v(c_sqh)
        tok_h = p.cnt["v"]

        for dh in range(2):
            for u8 in range(8):
                tok_w2 = [None]

                def c_ld2(g, u8=u8, dh=dh):
                    p.wait(g, "t", p.cnt["t"])
                    slot = p.dma_slot()
                    p.dgroup(g, slot)
                    tok_w2[0] = p.dma(
                        g, slot,
                        dns.rearrange("p (kt f) -> p kt f", f=512),
                        W["l1dn"][u8 * 512:(u8 + 1) * 512,
                                  dh * 512:(dh + 1) * 512]
                        .rearrange("(kt p) f -> p kt f", p=128))
                p.on_g(c_ld2)

                def c_d(t, u8=u8, dh=dh, tw=tok_w2[0], tok_h=tok_h):
                    def f():
                        p.wait(t, tw)
                        p.wait(t, "v", max(tok_h, p.cnt["v"]))
                        last = None
                        for ut in range(4):
                            for dt in range(4):
                                last = t.matmul(
                                    ps_o[:, dt * 512:(dt + 1) * 512],
                                    dns[:, ut * 512 + dt * 128:ut * 512 + (dt + 1) * 128],
                                    h_all[:, (u8 * 4 + ut) * 512:(u8 * 4 + ut + 1) * 512],
                                    start=(u8 == 0 and ut == 0),
                                    stop=(u8 == 7 and ut == 3))
                        return last
                    p.t(f)
                p.on_t(c_d)
            tok_d = p.cnt["t"]

            def c_add(v, dh=dh, lo=lo, tok_d=tok_d):
                p.wait(v, "t", tok_d)
                for dt in range(4):
                    d = dh * 4 + dt
                    sl = slice(d * NTOK + lo, d * NTOK + lo + 512)
                    p.v(v, lambda sl=sl, dt=dt: v.scalar_tensor_tensor(
                        X[:, sl], ps_o[:, dt * 512:(dt + 1) * 512], 0.5,
                        X[:, sl], ALU.mult, ALU.add))
            p.on_v(c_add)

    gateT = B["WG"]
    tok_gate_box = [None]

    def c_gl(g):
        p.dgroup(g, "d2")
        tok_gate_box[0] = p.dma(g, "d2",
                                gateT[:, :].rearrange("p (kt e) -> p kt e", e=8),
                                W["l1gate"].rearrange("(kt p) e -> p kt e", p=128))
    p.on_g(c_gl)
    tok_gate = tok_gate_box[0]

    SCEM = B["SC_EM"]
    tok_raw = emit_gate_scores(p, B, PS, gateT, tok_gate, xn_ready, SCEM)

    def c_sig_all(s):
        p.wait(s, "v", tok_raw)
        p.s(s, lambda: s.activation(SCEM, SCEM, AF.Sigmoid))
    p.on_s(c_sig_all)
    tok_sig = p.cnt["s"]

    ps_trf = PS["o4"][:, 0:128]
    wr_bf = B["WRBF"]
    SCtm = RT[:, 0:8]
    SFC = RT[:, 56:64]
    W1O = RT[:, 8:16]
    W2O = RT[:, 16:24]
    WV = RT[:, 24:32]
    rt = {"r0": RT[:, 32:33], "r1": RT[:, 33:34],
          "m1": RT[:, 34:35], "m2": RT[:, 35:36],
          "w0": RT[:, 40:48], "w1": RT[:, 48:56]}
    gs0 = RT[:, 36:37]
    gs1 = RT[:, 37:38]
    grp1 = RT[:, 38:39]
    den = RT[:, 39:40]
    CORRTM = B["CORRTM"]
    wvf = B["WVF"]

    def c_w0(v):
        p.v(v, lambda: v.memset(wvf, 0.0))
    p.on_v(c_w0)
    tok_w0 = p.cnt["v"]

    def half_top2(p, v, Gs, IOs, hot, w0, w1, r0, r1, msum):
        p.v(v, lambda: v.reduce_max(r0, Gs, AX.X))
        p.v(v, lambda: v.tensor_scalar(w0, Gs, r0, None, ALU.is_equal))
        p.v(v, lambda: v.tensor_scalar(w1, w0, -BIG, BIG, ALU.mult, ALU.add))
        p.v(v, lambda: v.scalar_tensor_tensor(w0, IOs, 1.0, w0, ALU.mult, ALU.mult))
        p.v(v, lambda: v.tensor_add(w1, w1, w0))
        p.v(v, lambda: v.tensor_reduce(r1, w1, AX.X, ALU.min))
        p.v(v, lambda: v.tensor_scalar(hot, IOs, r1, None, ALU.is_equal))
        p.v(v, lambda: v.scalar_tensor_tensor(w0, hot, -BIG, Gs, ALU.mult, ALU.add))
        p.v(v, lambda: v.reduce_max(r1, w0, AX.X))
        p.v(v, lambda: v.tensor_add(msum, r0, r1))

    for ch in range(NCH):
        lo = ch * CH

        def c_tr(t, lo=lo):
            def f():
                p.wait(t, "s", tok_sig)
                p.wait(t, "v", max(tok_w0, p.cnt["v"]))
                p.wait(t, B["tok_consts"][0])
                return t.transpose(ps_trf[:, 0:8], SCEM[:, lo:lo + CH],
                                   identf[0:8, 0:8])
            p.t(f)
        p.on_t(c_tr)
        tok_t = p.cnt["t"]

        def c_route(v, tok_t=tok_t):
            p.wait(v, "t", tok_t)
            p.wait(v, B["tok_consts"][0])
            p.wait(v, B["tok_corr"][0])
            p.v(v, lambda: v.tensor_copy(SCtm, ps_trf[:, 0:8]))
            p.v(v, lambda: v.tensor_add(SFC, SCtm, CORRTM))
            half_top2(p, v, SFC[:, 0:4], IO8[:, 0:4], rt["w0"][:, 0:4],
                      rt["w0"][:, 4:8], rt["w1"][:, 0:4], rt["r0"], rt["r1"], gs0)
            half_top2(p, v, SFC[:, 4:8], IO8[:, 0:4], rt["w0"][:, 0:4],
                      rt["w0"][:, 4:8], rt["w1"][:, 0:4], rt["r0"], rt["r1"], gs1)
            p.v(v, lambda: v.tensor_tensor(grp1, gs1, gs0, ALU.is_gt))
            p.v(v, lambda: v.memset(rt["w1"], 1.0))
            p.v(v, lambda: v.tensor_scalar(rt["w1"][:, 0:4], rt["w1"][:, 0:4],
                                           grp1, None, ALU.subtract))
            p.v(v, lambda: v.tensor_scalar(rt["w1"][:, 4:8], rt["w1"][:, 4:8],
                                           grp1, None, ALU.mult))
            p.v(v, lambda: v.tensor_mul(SFC, SFC, rt["w1"]))
            emit_top2(p, v, SFC, IO8, W1O, W2O, rt)
            p.v(v, lambda: v.tensor_mul(rt["w0"], W1O, SCtm))
            p.v(v, lambda: v.tensor_reduce(rt["m1"], rt["w0"], AX.X, ALU.add))
            p.v(v, lambda: v.tensor_mul(rt["w0"], W2O, SCtm))
            p.v(v, lambda: v.tensor_reduce(rt["m2"], rt["w0"], AX.X, ALU.add))
            p.v(v, lambda: v.tensor_add(den, rt["m1"], rt["m2"]))
            p.v(v, lambda: v.tensor_scalar_add(den, den, 1e-20))
            p.v(v, lambda: v.reciprocal(den, den))
            p.v(v, lambda: v.tensor_mul(rt["m1"], rt["m1"], den))
            p.v(v, lambda: v.tensor_mul(rt["m2"], rt["m2"], den))
            p.v(v, lambda: v.tensor_scalar(WV, W1O, rt["m1"], None, ALU.mult))
            p.v(v, lambda: v.tensor_scalar(rt["w0"], W2O, rt["m2"], None, ALU.mult))
            p.v(v, lambda: v.tensor_add(WV, WV, rt["w0"]))
            p.v(v, lambda: v.tensor_copy(wvf[:, 0:8], WV))
        p.on_v(c_route)
        tok_v2 = p.cnt["v"]

        def c_trb(t, tok_v2=tok_v2):
            def f():
                p.wait(t, "v", max(tok_v2, p.cnt["v"]))
                return t.transpose(ps_trf[:, :], wvf, identf)
            p.t(f)
        p.on_t(c_trb)
        tok_t2 = p.cnt["t"]

        def c_wrow(v, lo=lo, tok_t2=tok_t2):
            p.wait(v, "t", tok_t2)
            p.v(v, lambda: v.tensor_copy(wr_bf[:, lo:lo + CH], ps_trf[0:8, 0:CH]))
        p.on_v(c_wrow)
    tok_wrbf = p.cnt["v"]

    fc1s = PA[:, 0:4096]
    lat = PB[:, 0:8192]
    comb = PB[:, 8192:16384]
    tok_fc1 = [None]

    def c_lf(g):
        p.wait(g, "t", p.cnt["t"])
        slot = p.dma_slot()
        p.dgroup(g, slot)
        tok_fc1[0] = p.dma(g, slot,
                           fc1s.rearrange("p (kt f) -> p kt f", f=512),
                           W["l1fc1"].rearrange("(kt p) f -> p kt f", p=128))
    p.on_g(c_lf)

    for lt in range(4):
        for c in range(4):
            lo = c * 512

            def c_lat(t, lt=lt, lo=lo, tw=tok_fc1[0]):
                def f():
                    p.wait(t, tw)
                    p.wait(t, "v", max(xn_ready, p.cnt["v"]))
                    last = None
                    for d in range(NDT):
                        last = t.matmul(
                            ps_a[:, :],
                            fc1s[:, d * 512 + lt * 128:d * 512 + (lt + 1) * 128],
                            XN[:, d * NTOK + lo:d * NTOK + lo + 512],
                            start=(d == 0), stop=(d == NDT - 1))
                    return last
                p.t(f)
            p.on_t(c_lat)
            tok_t = p.cnt["t"]

            def c_ev(v, lt=lt, lo=lo, tok_t=tok_t):
                p.wait(v, "t", tok_t)
                p.v(v, lambda: v.tensor_copy(
                    lat[:, lt * NTOK + lo:lt * NTOK + lo + 512], ps_a[:, :]))
            p.on_v(c_ev)
    tok_lat = p.cnt["v"]

    wexp = [(PA[:, 0:2048], PA[:, 2048:4096]),
            (PA[:, 4096:6144], PA[:, 6144:8192])]
    hb = PA[:, 8192:10240]
    wbc = PC[:, 0:NTOK]
    t2 = PC[:, 3072:3584]
    ps_u = [ps_a, PS["c"]]
    t1_s = [PC[:, 2048:2560], PC[:, 2560:3072]]
    ps_bcb = PS["bc"]
    we_user = {0: p.cnt["t"], 1: p.cnt["t"]}
    we_ld = {}
    sqe_tok = {0: p.cnt["v"], 1: p.cnt["v"]}
    ho_tok = [p.cnt["t"]]
    comb_tok = [p.cnt["v"]]

    first = [True]
    for e in range(8):
        for nch in range(4):
            lo = nch * 512

            def c_bc(t, e=e, lo=lo):
                def f():
                    p.wait(t, "v", max(tok_wrbf, p.cnt["v"]))
                    p.wait(t, B["tok_consts"][0])
                    return t.matmul(ps_bcb[:, :], B["ESEL"][:, e * 128:(e + 1) * 128],
                                    wr_bf[:, lo:lo + 512], start=True, stop=True)
                p.t(f)
            p.on_t(c_bc)
            tok_t = p.cnt["t"]

            def c_bev(v, lo=lo, tok_t=tok_t):
                p.wait(v, "t", tok_t)
                p.v(v, lambda: v.tensor_copy(wbc[:, lo:lo + 512], ps_bcb[:, :]))
            p.on_v(c_bev)
        tok_wbc = p.cnt["v"]

        ep = e % 2
        w1s, w2s = wexp[ep]

        def c_lw(g, e=e, ep=ep, uw=we_user[ep]):
            p.wait(g, "t", uw if e >= 2 else p.cnt["t"])
            slot = p.dma_slot()
            p.dgroup(g, slot)
            p.dma(g, slot, w1s.rearrange("p (kt f) -> p kt f", f=512),
                  W["l1w1"][e].rearrange("(kt p) f -> p kt f", p=128))
            we_ld[ep] = p.dma(g, slot, w2s.rearrange("p (kt f) -> p kt f", f=512),
                              W["l1w2"][e].rearrange("(kt p) f -> p kt f", p=128))
        p.on_g(c_lw)

        for c in range(4):
            lo = c * 512
            for it in range(4):
                upar = it % 2

                def c_a(t, it=it, lo=lo, upar=upar, w1s=w1s,
                        tw=we_ld[ep], tok_lat=tok_lat,
                        sq=max(sqe_tok[upar], tok_wbc)):
                    def f():
                        p.wait(t, tw)
                        p.wait(t, "v", max(tok_lat, sq))
                        last = None
                        for ltk in range(4):
                            last = t.matmul(
                                ps_u[upar][:, :],
                                w1s[:, ltk * 512 + it * 128:ltk * 512 + (it + 1) * 128],
                                lat[:, ltk * NTOK + lo:ltk * NTOK + lo + 512],
                                start=(ltk == 0), stop=(ltk == 3))
                        return last
                    p.t(f)
                p.on_t(c_a)
                tok_a = p.cnt["t"]

                def c_r(s, upar=upar, tok_a=tok_a, sq=sqe_tok[upar]):
                    p.wait(s, "t", tok_a)
                    p.wait(s, "v", sq)
                    p.s(s, lambda: s.activation(t1_s[upar], ps_u[upar][:, :], AF.Relu))
                p.on_s(c_r)
                tok_r = p.cnt["s"]

                def c_h(v, it=it, lo=lo, upar=upar, tok_r=tok_r,
                        tok_wbc=tok_wbc, how=ho_tok[0]):
                    p.wait(v, "s", tok_r)
                    p.wait(v, "v", tok_wbc)
                    p.wait(v, "t", how)
                    p.v(v, lambda: v.tensor_mul(t2, t1_s[upar], wbc[:, lo:lo + 512]))
                    p.v(v, lambda: v.tensor_mul(
                        hb[:, it * 512:(it + 1) * 512], t1_s[upar], t2))
                p.on_v(c_h)
                sqe_tok[upar] = p.cnt["v"]
            tok_h = p.cnt["v"]

            def c_o(t, tok_h=tok_h, w2s=w2s, tw=we_ld[ep], cw=comb_tok[0]):
                def f():
                    p.wait(t, tw)
                    p.wait(t, "v", max(tok_h, cw))
                    last = None
                    for it in range(4):
                        for ltk in range(4):
                            last = t.matmul(
                                ps_o[:, ltk * 512:(ltk + 1) * 512],
                                w2s[:, it * 512 + ltk * 128:it * 512 + (ltk + 1) * 128],
                                hb[:, it * 512:(it + 1) * 512],
                                start=(it == 0), stop=(it == 3))
                    return last
                p.t(f)
            p.on_t(c_o)
            tok_o = p.cnt["t"]
            ho_tok[0] = tok_o

            def c_acc(v, lo=lo, tok_o=tok_o, fw=first[0]):
                p.wait(v, "t", tok_o)
                for ltk in range(4):
                    dst = comb[:, ltk * NTOK + lo:ltk * NTOK + lo + 512]
                    src = ps_o[:, ltk * 512:(ltk + 1) * 512]
                    if fw:
                        p.v(v, lambda dst=dst, src=src: v.tensor_copy(dst, src))
                    else:
                        p.v(v, lambda dst=dst, src=src: v.tensor_add(dst, dst, src))
            p.on_v(c_acc)
            comb_tok[0] = p.cnt["v"]
        we_user[ep] = p.cnt["t"]
        first[0] = False
    tok_comb = p.cnt["v"]

    fc2s = PA[:, 0:4096]
    tok_fc2 = [None]

    def c_lf2(g):
        p.wait(g, "t", p.cnt["t"])
        slot = p.dma_slot()
        p.dgroup(g, slot)
        tok_fc2[0] = p.dma(g, slot,
                           fc2s.rearrange("p (kt f) -> p kt f", f=1024),
                           W["l1fc2"].rearrange("(kt p) f -> p kt f", p=128))
    p.on_g(c_lf2)

    for c in range(4):
        lo = c * 512
        for dh in range(2):
            def c_f2(t, dh=dh, lo=lo, tw=tok_fc2[0], tok_comb=tok_comb):
                def f():
                    p.wait(t, tw)
                    p.wait(t, "v", max(tok_comb, p.cnt["v"]))
                    last = None
                    for ltk in range(4):
                        for dt in range(4):
                            last = t.matmul(
                                ps_o[:, dt * 512:(dt + 1) * 512],
                                fc2s[:, ltk * 1024 + dh * 512 + dt * 128:
                                     ltk * 1024 + dh * 512 + (dt + 1) * 128],
                                comb[:, ltk * NTOK + lo:ltk * NTOK + lo + 512],
                                start=(ltk == 0), stop=(ltk == 3))
                    return last
                p.t(f)
            p.on_t(c_f2)
            tok_t = p.cnt["t"]

            def c_add(v, dh=dh, lo=lo, tok_t=tok_t):
                p.wait(v, "t", tok_t)
                for dt in range(4):
                    d = dh * 4 + dt
                    sl = slice(d * NTOK + lo, d * NTOK + lo + 512)
                    p.v(v, lambda sl=sl, dt=dt: v.scalar_tensor_tensor(
                        X[:, sl], ps_o[:, dt * 512:(dt + 1) * 512], 0.5,
                        X[:, sl], ALU.mult, ALU.add))
            p.on_v(c_add)




def emit_l1_attn(p, B, W, PS, groups):
    X, XN = B["X"], B["XN"]
    def c_b0(g):
        p.dgroup(g, "d1")
        p.dma(g, "d1", B["BKV"], W["l1bkv"])
        B["tok_bkv"][0] = p.dma(g, "d1", B["BQO"], W["l1bqo"])
    p.on_g(c_b0)
    PA, PB, PC = B["PA"], B["PB"], B["PC"]
    FT = B["FT"]
    ps_a, ps_b, ps_o = PS["a"], PS["b"], PS["o4"]
    nc = p.nc

    xn_ready = emit_ln(p, B, PS)

    wq = PA[:, 0:8192]
    wk = PB[:, 0:8192]
    wv = PB[:, 8192:16384]
    kc = PC[:, 0:1024]
    vc = PC[:, 1024:2048]
    KtVb = PC[:, 2048:3072]
    qc = PC[:, 3072:4096]
    aoc = PC[:, 4096:5120]
    BKV = B["BKV"]
    BQ = B["BQO"][:, 0:8]
    BO = B["BQO"][:, 8:16]
    KVSB = FT[:, 0:1024]

    cc_in = nc.dram_tensor(f"cc_in_{p.cnt['t']}", [128, 1024], F32)
    cc_out = nc.dram_tensor(f"cc_out_{p.cnt['t']}", [128, 1024], F32)
    ccsem = p.st.enter_context(nc.semaphore(f"ccs_{p.cnt['t']}"))

    tok_wqkv = [None]

    def c_lw(g):
        p.wait(g, "t", p.cnt["t"])
        p.wait(g, "v", p.cnt["v"])
        slot = p.dma_slot()
        p.dgroup(g, slot)
        p.dma(g, slot, wq.rearrange("p (kt f) -> p kt f", f=1024),
              W["l1wq"].rearrange("(kt p) f -> p kt f", p=128))
        p.dma(g, slot, wk.rearrange("p (kt f) -> p kt f", f=1024),
              W["l1wk"].rearrange("(kt p) f -> p kt f", p=128))
        tok_wqkv[0] = p.dma(g, slot, wv.rearrange("p (kt f) -> p kt f", f=1024),
                            W["l1wv"].rearrange("(kt p) f -> p kt f", p=128))
    p.on_g(c_lw)
    tok_w = tok_wqkv[0]

    for ch in range(NCH):
        lo = ch * CH
        for half, pst in ((0, ps_a), (1, ps_b)):
            def c_k(t, lo=lo, half=half, pst=pst):
                def f():
                    p.wait(t, tok_w)
                    p.wait(t, B["tok_bkv"][0])
                    p.wait(t, "v", max(xn_ready, p.cnt["v"]))
                    last = None
                    for d in range(NDT):
                        last = t.matmul(
                            pst[:, :],
                            XN[:, d * NTOK + lo:d * NTOK + lo + CH],
                            wk[:, d * 1024 + half * 512:d * 1024 + (half + 1) * 512],
                            start=(d == 0), stop=(d == NDT - 1))
                    return last
                p.t(f)
            p.on_t(c_k)
            tok_k = p.cnt["t"]

            def c_ke(v, half=half, pst=pst, tok_k=tok_k):
                p.wait(v, "t", tok_k)
                p.v(v, lambda: v.tensor_add(
                    kc[:, half * 512:(half + 1) * 512], pst[:, :],
                    BKV[:, half * 512:(half + 1) * 512]))
            p.on_v(c_ke)

        for half, pst in ((0, ps_a), (1, ps_b)):
            def c_v(t, lo=lo, half=half, pst=pst):
                def f():
                    p.wait(t, "v", p.cnt["v"])
                    last = None
                    for d in range(NDT):
                        last = t.matmul(
                            pst[:, :],
                            XN[:, d * NTOK + lo:d * NTOK + lo + CH],
                            wv[:, d * 1024 + half * 512:d * 1024 + (half + 1) * 512],
                            start=(d == 0), stop=(d == NDT - 1))
                    return last
                p.t(f)
            p.on_t(c_v)
            tok_v = p.cnt["t"]

            def c_ve(v, half=half, pst=pst, tok_v=tok_v):
                p.wait(v, "t", tok_v)
                p.v(v, lambda: v.tensor_add(
                    vc[:, half * 512:(half + 1) * 512], pst[:, :],
                    BKV[:, 1024 + half * 512:1024 + (half + 1) * 512]))
            p.on_v(c_ve)
        tok_kv = p.cnt["v"]

        def c_ktv(t, ch=ch, tok_kv=tok_kv):
            def f():
                p.wait(t, "v", max(tok_kv, p.cnt["v"]))
                last = None
                for h in range(8):
                    last = t.matmul(
                        ps_o[:, h * 128:(h + 1) * 128],
                        kc[:, h * 128:(h + 1) * 128],
                        vc[:, h * 128:(h + 1) * 128],
                        start=(ch == 0 and h % 4 == 0),
                        stop=(ch == NCH - 1 and h % 4 == 3))
                return last
            p.t(f)
        p.on_t(c_ktv)

    tok_ktv = p.cnt["t"]

    def c_kve(v):
        p.wait(v, "t", tok_ktv)
        p.v(v, lambda: v.tensor_copy(KVSB, ps_o[:, 0:1024]))
    p.on_v(c_kve)
    tok_kvsb = p.cnt["v"]

    def c_cc(g):
        p.wait(g, "v", tok_kvsb)
        p.dgroup(g, "d2")
        t1 = p.dma(g, "d2", cc_in.ap(), KVSB)
        p.wait(g, t1)
        g.collective_compute(
            "AllReduce", ALU.add, replica_groups=groups,
            ins=[cc_in.ap()], outs=[cc_out.ap()]).then_inc(ccsem)
        g.wait_ge(ccsem, 1)
        t2 = p.dma(g, "d2", KVSB, cc_out.ap())
        p.wait(g, t2)
        p.g(g, lambda: g.memset(B["RT"][0:1, 63:64], 0.0))
    p.on_g(c_cc)
    tok_cc = p.cnt["g"]

    def c_kvb(v):
        p.wait(v, "g", tok_cc)
        p.v(v, lambda: v.tensor_copy(KtVb, KVSB))
    p.on_v(c_kvb)
    tok_kvb = p.cnt["v"]

    wo = PB[:, 0:8192]
    tok_wo_box = [None]

    def c_lwo(g):
        p.wait(g, "t", p.cnt["t"])
        slot = p.dma_slot()
        p.dgroup(g, slot)
        tok_wo_box[0] = p.dma(g, slot, wo.rearrange("p (kt f) -> p kt f", f=1024),
                              W["l1wo"].rearrange("(kt p) f -> p kt f", p=128))
    p.on_g(c_lwo)
    tok_wo = tok_wo_box[0]

    for ch in range(NCH):
        lo = ch * CH
        for dt in range(NDT):
            def c_q(t, dt=dt, lo=lo):
                def f():
                    p.wait(t, tok_w)
                    p.wait(t, "v", p.cnt["v"])
                    last = None
                    for d in range(NDT):
                        last = t.matmul(
                            ps_a[:, 0:128],
                            wq[:, d * 1024 + dt * 128:d * 1024 + (dt + 1) * 128],
                            XN[:, d * NTOK + lo:d * NTOK + lo + CH],
                            start=(d == 0), stop=(d == NDT - 1))
                    return last
                p.t(f)
            p.on_t(c_q)
            tok_q = p.cnt["t"]

            def c_qe(v, dt=dt, tok_q=tok_q):
                p.wait(v, "t", tok_q)
                p.wait(v, B["tok_bkv"][0])
                p.v(v, lambda: v.tensor_scalar(qc[:, dt * 128:(dt + 1) * 128],
                                               ps_a[:, 0:128], BQ[:, dt:dt + 1],
                                               None, ALU.add))
            p.on_v(c_qe)
        tok_qc = p.cnt["v"]

        for h in range(8):
            def c_oh(t, h=h, tok_qc=tok_qc):
                def f():
                    p.wait(t, "v", tok_qc)
                    p.wait(t, "v", max(tok_kvb, p.cnt["v"]))
                    return t.matmul(ps_b[:, 0:128],
                                    KtVb[:, h * 128:(h + 1) * 128],
                                    qc[:, h * 128:(h + 1) * 128],
                                    start=True, stop=True)
                p.t(f)
            p.on_t(c_oh)
            tok_oh = p.cnt["t"]

            def c_ohe(v, h=h, tok_oh=tok_oh):
                p.wait(v, "t", tok_oh)
                p.v(v, lambda: v.tensor_copy(aoc[:, h * 128:(h + 1) * 128],
                                             ps_b[:, 0:128]))
            p.on_v(c_ohe)
        tok_ao = p.cnt["v"]

        def c_wo(t, tok_ao=tok_ao):
            def f():
                p.wait(t, tok_wo)
                p.wait(t, "v", max(tok_ao, p.cnt["v"]))
                last = None
                for kt in range(8):
                    for dt in range(8):
                        last = t.matmul(
                            ps_o[:, 1024 + dt * 128:1024 + (dt + 1) * 128],
                            wo[:, kt * 1024 + dt * 128:kt * 1024 + (dt + 1) * 128],
                            aoc[:, kt * 128:(kt + 1) * 128],
                            start=(kt == 0 and dt % 4 == 0),
                            stop=(kt == 7 and dt % 4 == 3))
                return last
            p.t(f)
        p.on_t(c_wo)
        tok_t = p.cnt["t"]

        def c_res(v, lo=lo, tok_t=tok_t):
            p.wait(v, "t", tok_t)
            p.wait(v, B["tok_bkv"][0])
            for dt in range(NDT):
                sl = slice(dt * NTOK + lo, dt * NTOK + lo + CH)
                p.v(v, lambda sl=sl, dt=dt: v.scalar_tensor_tensor(
                    B["FT"][:, 1024 + dt * 0:1024 + CH] if False else X[:, sl],
                    ps_o[:, 1024 + dt * 128:1024 + (dt + 1) * 128],
                    BO[:, dt:dt + 1], X[:, sl], ALU.add, ALU.add))
        p.on_v(c_res)




def _rope_phi(p, B, src, scr, dst, ch, tok_pre):
    CTS, STS = B["CTS"], B["STS"]
    ct3 = CTS[:, ch * 32:(ch + 1) * 32].unsqueeze(1).broadcast_to((128, 16, 32))
    st3 = STS[:, ch * 32:(ch + 1) * 32].unsqueeze(1).broadcast_to((128, 16, 32))
    v4 = lambda ap: ap.rearrange("p (h t j) -> p h t j", t=2, j=32)
    x1 = v4(src)[:, :, 0, :]
    x2 = v4(src)[:, :, 1, :]
    s1 = v4(scr)[:, :, 0, :]
    s2 = v4(scr)[:, :, 1, :]

    def c_rope(v):
        p.wait(v, B["tok_ctab"][0])
        p.v(v, lambda: v.tensor_tensor(s1, x1, ct3, ALU.mult))
        p.v(v, lambda: v.tensor_tensor(s2, x2, ct3, ALU.mult))
        p.v(v, lambda: v.tensor_tensor(x2, x2, st3, ALU.mult))
        p.v(v, lambda: v.tensor_sub(s1, s1, x2))
        p.v(v, lambda: v.tensor_tensor(x1, x1, st3, ALU.mult))
        p.v(v, lambda: v.tensor_add(s2, s2, x1))
        p.v(v, lambda: v.tensor_scalar_min(src, scr, 0.0))
    p.on_v(c_rope)
    tok_m = p.cnt["v"]

    def c_exp(sc):
        p.wait(sc, "v", tok_m)
        p.s(sc, lambda: sc.activation(src, src, AF.Exp))
    p.on_s(c_exp)
    tok_e = p.cnt["s"]

    def c_phi(v):
        p.wait(v, "s", tok_e)
        p.v(v, lambda: v.scalar_tensor_tensor(dst, scr, 0.0, src, ALU.max, ALU.add))
    p.on_v(c_phi)
    return p.cnt["v"]


def emit_l0_attn(p, B, W, PS, groups):
    X, XN = B["X"], B["XN"]
    PA, PB, PC = B["PA"], B["PB"], B["PC"]
    FT = B["FT"]
    XT = B["XTRA"]
    ps_a, ps_b, ps_o, ps_tr = PS["a"], PS["b"], PS["o4"], PS["trp"]
    nc = p.nc
    onescb = B["onesc"]
    identb = B["ident"]

    xn_ready = emit_ln(p, B, PS)

    wq = PA[:, 0:8192]
    wk = PB[:, 0:8192]
    wv = PB[:, 8192:16384]
    wo_lo = PA[:, 8192:14336]
    wo_hi = XT[:, 2048:4096]

    kc = PC[:, 0:1024]
    vc = PC[:, 1024:2048]
    qc = PC[:, 2048:3072]
    kF = PC[:, 3072:4096]
    qF = PC[:, 4096:5120]
    Yc = PC[:, 5120:6144]
    ZB = PC[:, 6144:6656]
    AMB = XT[:, 1024:1280]
    KVB = XT[:, 1280:1408]
    RDB = XT[:, 1408:1536]
    DBC = XT[:, 1536:1664]
    PSL = XT[:, 1664:1792]
    TRIB = XT[:, 1792:1920]
    B["CTS"] = XT[:, 0:512]
    B["STS"] = XT[:, 512:1024]
    kf32 = FT[:, 0:1024]
    qf32 = FT[:, 1024:2048]
    KVSB = FT[:, 0:1024]
    STV = B["STV"]
    KVMASK = B["KVMASK"]

    nrank = len(groups[0])
    cc_in = nc.dram_tensor(f"cc0_in_{p.cnt['t']}", [128, 1024], F32)
    cc_ag = nc.dram_tensor(f"cc0_ag_{p.cnt['t']}", [nrank * 128, 1024], F32)
    ccsem = p.st.enter_context(nc.semaphore(f"ccs0_{p.cnt['t']}"))

    tok_w_box = [None]

    def c_lw(g):
        p.wait(g, "t", p.cnt["t"])
        p.wait(g, "v", p.cnt["v"])
        slot = p.dma_slot()
        p.dgroup(g, slot)
        p.dma(g, slot, wk.rearrange("p (kt f) -> p kt f", f=1024),
              W["l0wk"].rearrange("(kt p) f -> p kt f", p=128))
        p.dma(g, slot, wv.rearrange("p (kt f) -> p kt f", f=1024),
              W["l0wv"].rearrange("(kt p) f -> p kt f", p=128))
        tok_w_box[0] = p.dma(g, slot, wq.rearrange("p (kt f) -> p kt f", f=1024),
                             W["l0wq"].rearrange("(kt p) f -> p kt f", p=128))
    p.on_g(c_lw)
    tok_w = tok_w_box[0]

    def c_lc(g):
        p.dgroup(g, "d1")
        p.dma(g, "d1", B["CTS"], W["l0ctab"])
        p.dma(g, "d1", B["STS"], W["l0stab"])
        p.dma(g, "d1", TRIB, W["l0tri"])
        p.dma(g, "d1", PSL, W["l0psel"])
        B["tok_ctab"][0] = p.dma(g, "d1", KVMASK, W["l0kvmask"])
    p.on_g(c_lc)

    def c_z(v):
        p.v(v, lambda: v.memset(RDB, 0.0))
        p.v(v, lambda: v.memset(ZB, 0.0))
        p.v(v, lambda: v.memset(STV, 0.0))
    p.on_v(c_z)

    def proj_chunk(wmat, lo, dst_f32):
        for half, pst in ((0, ps_a), (1, ps_b)):
            def c_p(t, half=half, pst=pst, lo=lo, wmat=wmat):
                def f():
                    p.wait(t, tok_w)
                    p.wait(t, "v", max(xn_ready, p.cnt["v"]))
                    last = None
                    for d in range(NDT):
                        last = t.matmul(
                            pst[:, :],
                            XN[:, d * NTOK + lo:d * NTOK + lo + CH],
                            wmat[:, d * 1024 + half * 512:d * 1024 + (half + 1) * 512],
                            start=(d == 0), stop=(d == NDT - 1))
                    return last
                p.t(f)
            p.on_t(c_p)
            tok_t = p.cnt["t"]

            def c_e(v, half=half, pst=pst, tok_t=tok_t, dst=dst_f32):
                p.wait(v, "t", tok_t)
                p.v(v, lambda: v.tensor_copy(
                    dst[:, half * 512:(half + 1) * 512], pst[:, :]))
            p.on_v(c_e)

    def proj_v(lo):
        for half, pst in ((0, ps_a), (1, ps_b)):
            def c_pv(t, half=half, pst=pst, lo=lo):
                def f():
                    p.wait(t, "v", p.cnt["v"])
                    last = None
                    for d in range(NDT):
                        last = t.matmul(
                            pst[:, :],
                            XN[:, d * NTOK + lo:d * NTOK + lo + CH],
                            wv[:, d * 1024 + half * 512:d * 1024 + (half + 1) * 512],
                            start=(d == 0), stop=(d == NDT - 1))
                    return last
                p.t(f)
            p.on_t(c_pv)
            tok_t = p.cnt["t"]

            def c_ev(v, half=half, pst=pst, tok_t=tok_t):
                p.wait(v, "t", tok_t)
                p.v(v, lambda: v.tensor_copy(
                    vc[:, half * 512:(half + 1) * 512], pst[:, :]))
            p.on_v(c_ev)

    def state_update_group(ch, phase):
        tok_kv = p.cnt["v"]

        def c_upd(t, tok_kv=tok_kv):
            def f():
                p.wait(t, "v", max(tok_kv, p.cnt["v"]))
                p.wait(t, B["tok_consts"][0])
                last = None
                for bank in range(2):
                    t.matmul(ps_o[:, bank * 512:(bank + 1) * 512], identb, ZB,
                             start=True, stop=False)
                for h in range(16):
                    co = (h // 2 % 4) * 128 + (h // 8) * 512
                    po = (h % 2) * 64
                    t.matmul(ps_o[po:po + 64, co:co + 64],
                             kc[:, h * 64:(h + 1) * 64],
                             vc[:, h * 64:(h + 1) * 64],
                             start=False, stop=False)
                    last = t.matmul(ps_o[po:po + 64, co + 64:co + 65],
                                    kc[:, h * 64:(h + 1) * 64],
                                    onescb, start=False, stop=False)
                for bank in range(2):
                    last = t.matmul(ps_o[:, bank * 512:(bank + 1) * 512],
                                    identb, ZB, start=False, stop=True)
                return last
            p.t(f)
        p.on_t(c_upd)
        tok_u = p.cnt["t"]

        def c_acc(v, tok_u=tok_u):
            p.wait(v, "t", tok_u)
            p.v(v, lambda: v.tensor_add(STV[:, 0:512], STV[:, 0:512],
                                        ps_o[:, 0:512]))
            p.v(v, lambda: v.tensor_add(STV[:, 512:1024], STV[:, 512:1024],
                                        ps_o[:, 512:1024]))
        p.on_v(c_acc)

    for ch in range(NCH):
        lo = ch * CH
        proj_chunk(wk, lo, kf32)
        _rope_phi(p, B, kf32, qf32, kc, ch, None)
        proj_v(lo)
        state_update_group(ch, "A")

    tok_tot = p.cnt["v"]

    def c_cc(g):
        p.wait(g, "v", tok_tot)
        p.dgroup(g, "d2")
        t1 = p.dma(g, "d2", cc_in.ap(), STV)
        p.wait(g, t1)
        g.collective_compute(
            "AllGather", ALU.bypass, replica_groups=groups,
            ins=[cc_in.ap()], outs=[cc_ag.ap()]).then_inc(ccsem)
        g.wait_ge(ccsem, 1)
    p.on_g(c_cc)

    def c_st0(v):
        p.wait(v, "d2", p.cnt["d2"])
        v.wait_ge(ccsem, 1)
        p.v(v, lambda: v.memset(STV, 0.0))
    p.on_v(c_st0)
    for r in range(nrank):
        def c_ld(g, r=r):
            p.wait(g, "v", p.cnt["v"])
            g.wait_ge(ccsem, 1)
            p.dgroup(g, "d2")
            p.dma(g, "d2", KVSB, cc_ag.ap()[r * 128:(r + 1) * 128, :])
        p.on_g(c_ld)
        tok_r = ("d2", p.cnt["d2"])

        def c_fold(v, r=r, tok_r=tok_r):
            p.wait(v, tok_r)
            p.wait(v, B["tok_ctab"][0])
            p.v(v, lambda: v.scalar_tensor_tensor(
                STV, KVSB, KVMASK[:, r:r + 1], STV, ALU.mult, ALU.add))
        p.on_v(c_fold)
    tok_prefix = p.cnt["v"]

    tok_wo_box = [None]

    def c_lwo(g):
        p.wait(g, "t", p.cnt["t"])
        slot = p.dma_slot()
        p.dgroup(g, slot)
        p.dma(g, slot, wo_lo.rearrange("p (kt f) -> p kt f", f=1024),
              W["l0wo"][0:768, :].rearrange("(kt p) f -> p kt f", p=128))
        tok_wo_box[0] = p.dma(g, slot,
                              wo_hi.rearrange("p (kt f) -> p kt f", f=1024),
                              W["l0wo"][768:1024, :]
                              .rearrange("(kt p) f -> p kt f", p=128))
    p.on_g(c_lwo)
    tok_wo = tok_wo_box[0]

    for ch in range(NCH):
        lo = ch * CH
        proj_chunk(wk, lo, kf32)
        _rope_phi(p, B, kf32, qf32, kc, ch, None)
        proj_v(lo)
        proj_chunk(wq, lo, qf32)
        _rope_phi(p, B, qf32, kf32, qc, ch, None)
        tok_q = p.cnt["v"]

        for which, (srcb, dstb) in enumerate(((qc, qF), (kc, kF))):
            for pt in range(8):
                def c_tr(t, srcb=srcb, pt=pt):
                    def f():
                        p.wait(t, "v", max(tok_q, p.cnt["v"]))
                        return t.transpose(ps_tr[:, :],
                                           srcb[:, pt * 128:(pt + 1) * 128],
                                           identb)
                    p.t(f)
                p.on_t(c_tr)
                tok_t = p.cnt["t"]

                def c_te2(v, dstb=dstb, pt=pt, tok_t=tok_t):
                    p.wait(v, "t", tok_t)
                    p.v(v, lambda: v.tensor_copy(
                        dstb[:, pt * 128:(pt + 1) * 128], ps_tr[:, :]))
                p.on_v(c_te2)
        tok_tr = p.cnt["v"]

        for pr in range(8):
            co = (pr % 4) * 128 + (pr // 4) * 512

            def c_kvb(v, co=co, tok_tr=tok_tr):
                p.wait(v, "v", max(tok_tr, tok_prefix))
                p.v(v, lambda: v.tensor_copy(KVB[:, 0:65], STV[:, co:co + 65]))
            p.on_v(c_kvb)
            tok_kvb = p.cnt["v"]

            def c_A(t, pr=pr, tok_kvb=tok_kvb):
                def f():
                    p.wait(t, "v", max(tok_kvb, p.cnt["v"]))
                    t.matmul(ps_a[:, 0:128], kF[0:64, pr * 128:(pr + 1) * 128],
                             qF[0:64, pr * 128:(pr + 1) * 128],
                             start=True, stop=True)
                    return t.matmul(ps_b[:, 0:128],
                                    kF[64:128, pr * 128:(pr + 1) * 128],
                                    qF[64:128, pr * 128:(pr + 1) * 128],
                                    start=True, stop=True)
                p.t(f)
            p.on_t(c_A)
            tok_A = p.cnt["t"]

            def c_am(v, tok_A=tok_A):
                p.wait(v, "t", tok_A)
                p.wait(v, B["tok_ctab"][0])
                p.v(v, lambda: v.tensor_tensor(AMB[:, 0:128], ps_a[:, 0:128],
                                               TRIB, ALU.mult))
                p.v(v, lambda: v.tensor_tensor(AMB[:, 128:256], ps_b[:, 0:128],
                                               TRIB, ALU.mult))
            p.on_v(c_am)
            tok_am = p.cnt["v"]

            def c_nd(t, pr=pr, tok_am=tok_am):
                def f():
                    p.wait(t, "v", max(tok_am, p.cnt["v"]))
                    nb = 1024
                    t.matmul(ps_o[:, nb:nb + 512], identb, ZB,
                             start=True, stop=False)
                    t.matmul(ps_o[0:64, nb:nb + 128],
                             vc[:, (2 * pr) * 64:(2 * pr + 1) * 64],
                             AMB[:, 0:128], start=False, stop=False)
                    t.matmul(ps_o[64:128, nb:nb + 128],
                             vc[:, (2 * pr + 1) * 64:(2 * pr + 2) * 64],
                             AMB[:, 128:256], start=False, stop=False)
                    t.matmul(ps_o[0:64, nb:nb + 128], KVB[0:64, 0:64],
                             qF[0:64, pr * 128:(pr + 1) * 128],
                             start=False, stop=False)
                    t.matmul(ps_o[64:128, nb:nb + 128], KVB[64:128, 0:64],
                             qF[64:128, pr * 128:(pr + 1) * 128],
                             start=False, stop=False)
                    t.matmul(ps_o[0:1, nb + 128:nb + 256], onescb,
                             AMB[:, 0:128], start=False, stop=False)
                    t.matmul(ps_o[64:65, nb + 128:nb + 256], onescb,
                             AMB[:, 128:256], start=False, stop=False)
                    t.matmul(ps_o[0:1, nb + 128:nb + 256], KVB[0:64, 64:65],
                             qF[0:64, pr * 128:(pr + 1) * 128],
                             start=False, stop=False)
                    t.matmul(ps_o[64:65, nb + 128:nb + 256],
                             KVB[64:128, 64:65],
                             qF[64:128, pr * 128:(pr + 1) * 128],
                             start=False, stop=False)
                    return t.matmul(ps_o[:, nb:nb + 512], identb, ZB,
                                    start=False, stop=True)
                p.t(f)
            p.on_t(c_nd)
            tok_nd = p.cnt["t"]

            def c_den2(v, tok_nd=tok_nd):
                p.wait(v, "t", tok_nd)
                nb = 1024
                drow0 = ps_o[0:1, nb + 128:nb + 256]
                drow1 = ps_o[64:65, nb + 128:nb + 256]
                f0 = B["DENF"][0:1, 0:128]
                f1 = B["DENF"][0:1, 128:256]
                p.v(v, lambda: v.tensor_scalar_max(f0, drow0, 1e-6))
                p.v(v, lambda: v.tensor_scalar_max(f1, drow1, 1e-6))
                p.v(v, lambda: v.reciprocal(f0, f0))
                p.v(v, lambda: v.reciprocal(f1, f1))
                p.v(v, lambda: v.tensor_copy(RDB[0:1, 0:128], f0))
                p.v(v, lambda: v.tensor_copy(RDB[64:65, 0:128], f1))
            p.on_v(c_den2)
            tok_den = p.cnt["v"]

            def c_dbc(t, tok_den=tok_den):
                def f():
                    p.wait(t, "v", max(tok_den, p.cnt["v"]))
                    p.wait(t, B["tok_ctab"][0])
                    return t.matmul(ps_a[:, 0:128], PSL, RDB,
                                    start=True, stop=True)
                p.t(f)
            p.on_t(c_dbc)
            tok_dbc = p.cnt["t"]

            def c_dbce(v, tok_dbc=tok_dbc):
                p.wait(v, "t", tok_dbc)
                p.v(v, lambda: v.tensor_copy(DBC, ps_a[:, 0:128]))
            p.on_v(c_dbce)

            def c_y(v, pr=pr):
                nb = 1024
                p.v(v, lambda: v.tensor_tensor(
                    Yc[:, pr * 128:(pr + 1) * 128],
                    ps_o[:, nb:nb + 128], DBC, ALU.mult))
            p.on_v(c_y)

        state_update_group(ch, "B")

        for dh in range(2):
            def c_wo(t, dh=dh):
                def f():
                    p.wait(t, tok_wo)
                    p.wait(t, "v", p.cnt["v"])
                    t.matmul(ps_o[:, 1536:2048], identb, ZB,
                             start=True, stop=False)
                    for kt in range(8):
                        wsrc = wo_lo if kt < 6 else wo_hi
                        wco = kt * 1024 if kt < 6 else (kt - 6) * 1024
                        for dt in range(4):
                            d = dh * 4 + dt
                            t.matmul(
                                ps_o[:, 1536 + dt * 128:1536 + (dt + 1) * 128],
                                wsrc[:, wco + d * 128:wco + (d + 1) * 128],
                                Yc[:, kt * 128:(kt + 1) * 128],
                                start=False, stop=False)
                    return t.matmul(ps_o[:, 1536:2048], identb, ZB,
                                    start=False, stop=True)
                p.t(f)
            p.on_t(c_wo)
            tok_t = p.cnt["t"]

            def c_res(v, dh=dh, lo=lo, tok_t=tok_t):
                p.wait(v, "t", tok_t)
                for dt in range(4):
                    d = dh * 4 + dt
                    sl = slice(d * NTOK + lo, d * NTOK + lo + CH)
                    p.v(v, lambda sl=sl, dt=dt: v.tensor_add(
                        X[:, sl], X[:, sl],
                        ps_o[:, 1536 + dt * 128:1536 + (dt + 1) * 128]))
            p.on_v(c_res)


def alloc_common(nc, st):
    sb = lambda n, s, d: st.enter_context(nc.sbuf_tensor(n, s, d))[:, :]
    B = {}
    B["X"] = sb("Xr", [128, NDT * NTOK], F32)
    B["XN"] = sb("XNr", [128, NDT * NTOK], BF16)
    B["PA"] = sb("PA", [128, 14336], BF16)
    B["PB"] = sb("PB", [128, 16384], BF16)
    B["PC"] = sb("PC", [128, 7168], BF16)
    B["FT"] = sb("FT", [128, 2048], F32)
    B["RT"] = sb("RT", [128, 64], F32)
    B["ROWF"] = sb("ROWF", [1, 2048], F32)
    B["SC_EM"] = B["FT"][0:8, 0:2048]
    B["WRBF"] = sb("WRBF", [8, 2048], BF16)
    B["WVF"] = sb("WVF", [128, 128], F32)
    B["ident"] = sb("identb", [128, 128], BF16)
    B["identf"] = sb("identf_sb", [128, 128], F32)
    B["onesc"] = sb("onescb", [128, 1], BF16)
    B["onesrf"] = sb("onesrf_sb", [1, 128], F32)
    B["onesr"] = sb("onesrb", [1, 128], BF16)
    B["iota8"] = sb("iota8f", [128, 8], F32)
    B["ESEL"] = sb("ESEL", [8, 1024], BF16)
    B["WG"] = sb("WG", [128, 64], BF16)
    B["EPS"] = sb("EPSc", [1, 1], F32)
    B["CORRTM"] = sb("CORRTM", [128, 8], F32)
    B["XTRA"] = sb("XTRA", [128, 4096], BF16)
    B["BKV"] = B["XTRA"][:, 0:2048]
    B["BQO"] = sb("BQO", [128, 16], F32)
    B["KVMASK"] = sb("KVMASK", [128, 4], F32)
    B["DENF"] = sb("DENF", [1, 256], F32)
    B["STV"] = sb("STV", [128, 1024], F32)
    B["tok_x"] = [None]
    B["tok_consts"] = [None]
    B["tok_corr"] = [None]
    B["tok_bkv"] = [None]
    B["tok_ctab"] = [None]
    return B


def alloc_psum(nc, st):
    psd = lambda n, s, d: st.enter_context(nc.psum_tensor(n, s, d))
    PS = {}
    PS["a"] = psd("psa", [128, 512], F32)
    PS["b"] = psd("psb", [128, 512], F32)
    PS["c"] = psd("psc", [128, 512], F32)
    PS["d"] = psd("psd", [128, 512], F32)
    PS["o4"] = psd("pso", [128, 2048], F32)
    PS["trp"] = PS["d"][:, :][:, 0:64].bitcast(BF16)
    PS["r1"] = PS["a"][0:1, :]
    PS["r2"] = PS["b"][0:1, :]
    PS["r8"] = PS["b"][0:8, :]
    PS["bc"] = PS["a"][:, :]
    return PS


def declare_common_inputs(nc):
    W = {}
    W["consts"] = nc.dram_tensor("consts", [128, 16], F32, kind="ExternalInput").ap()
    W["ident"] = nc.dram_tensor("identf", [128, 128], F32, kind="ExternalInput").ap()
    W["onesrow"] = nc.dram_tensor("onesrow", [1, 128], F32, kind="ExternalInput").ap()
    W["esel"] = nc.dram_tensor("esel", [8, 1024], F32, kind="ExternalInput").ap()
    return W


def emit_prelude(p, B, W, xin):
    def c_init(g):
        ctx = p.nc.allow_non_contiguous_dma(reason="tiny const loads")
        ctx.__enter__()
        B["tok_x"][0] = p.dma(
            g, "d0",
            B["X"][:, :].rearrange("p (dt n) -> p dt n", n=NTOK),
            xin.rearrange("(dt p) n -> p dt n", p=128))
        p.dma(g, "d1", B["onesc"], W["consts"][:, 0:1])
        p.dma(g, "d1", B["iota8"], W["consts"][:, 2:10])
        p.dma(g, "d1", B["ident"], W["ident"])
        p.dma(g, "d1", B["identf"], W["ident"])
        p.dma(g, "d1", B["onesr"], W["onesrow"])
        p.dma(g, "d1", B["onesrf"], W["onesrow"])
        p.dma(g, "d1", B["EPS"], W["consts"][0:1, 10:11])
        B["tok_consts"][0] = p.dma(g, "d1", B["ESEL"], W["esel"])
        ctx.__exit__(None, None, None)
    p.on_g(c_init)


def emit_store(p, xout, B):
    def c_store(q):
        p.wait(q, "v", p.cnt["v"])
        p.wait(q, "s", p.cnt["s"])
        tok = p.dma(q, "d5", xout.rearrange("(dt p) n -> p dt n", p=128),
                    B["X"][:, :].rearrange("p (dt n) -> p dt n", n=NTOK))
        q.wait_ge(p.sem[tok[0]], tok[1])
    p.on_q(c_store)


def build_l0_ffn_program():
    nc = bass.Bass("TRN2", target_bir_lowering=False, debug=False)
    st = ExitStack()
    p = Prog(nc, st)
    W = declare_common_inputs(nc)
    xin = nc.dram_tensor("x_in", [D, NTOK], F32, kind="ExternalInput").ap()
    xout = nc.dram_tensor("x_out", [D, NTOK], F32, kind="ExternalOutput").ap()
    W["l0gate"] = nc.dram_tensor("l0gate", [D, 8], F32, kind="ExternalInput").ap()
    W["l0w1"] = nc.dram_tensor("l0w1", [8, D, 2048], F32, kind="ExternalInput").ap()
    W["l0w3"] = nc.dram_tensor("l0w3", [8, D, 2048], F32, kind="ExternalInput").ap()
    W["l0w2"] = nc.dram_tensor("l0w2", [8, 2048, D], F32, kind="ExternalInput").ap()

    B = alloc_common(nc, st)
    PS = alloc_psum(nc, st)
    with nc.Block() as blk:
        p.blk = blk
        emit_prelude(p, B, W, xin)
        emit_l0_ffn(p, B, W, PS)
        emit_store(p, xout, B)
    st.close()
    return nc


def build_l1_ffn_program():
    nc = bass.Bass("TRN2", target_bir_lowering=False, debug=False)
    st = ExitStack()
    p = Prog(nc, st)
    W = declare_common_inputs(nc)
    xin = nc.dram_tensor("x_in", [D, NTOK], F32, kind="ExternalInput").ap()
    xout = nc.dram_tensor("x_out", [D, NTOK], F32, kind="ExternalOutput").ap()
    W["l1gate"] = nc.dram_tensor("l1gate", [D, 8], F32, kind="ExternalInput").ap()
    W["l1corrtm"] = nc.dram_tensor("l1corrtm", [128, 8], F32, kind="ExternalInput").ap()
    W["l1w1"] = nc.dram_tensor("l1w1", [8, 512, 512], F32, kind="ExternalInput").ap()
    W["l1w2"] = nc.dram_tensor("l1w2", [8, 512, 512], F32, kind="ExternalInput").ap()
    W["l1fc1"] = nc.dram_tensor("l1fc1", [D, 512], F32, kind="ExternalInput").ap()
    W["l1fc2"] = nc.dram_tensor("l1fc2", [512, D], F32, kind="ExternalInput").ap()
    W["l1up"] = nc.dram_tensor("l1up", [D, 4096], F32, kind="ExternalInput").ap()
    W["l1dn"] = nc.dram_tensor("l1dn", [4096, D], F32, kind="ExternalInput").ap()

    B = alloc_common(nc, st)
    PS = alloc_psum(nc, st)
    with nc.Block() as blk:
        p.blk = blk
        emit_prelude(p, B, W, xin)

        def c_corr(g):
            p.dgroup(g, "d1")
            B["tok_corr"][0] = p.dma(g, "d1", B["CORRTM"], W["l1corrtm"])
        p.on_g(c_corr)

        emit_l1_ffn(p, B, W, PS)
        emit_store(p, xout, B)
    st.close()
    return nc


def build_l1_attn_program(groups):
    nc = bass.Bass("TRN2", target_bir_lowering=False, debug=False)
    st = ExitStack()
    p = Prog(nc, st)
    W = declare_common_inputs(nc)
    xin = nc.dram_tensor("x_in", [D, NTOK], F32, kind="ExternalInput").ap()
    xout = nc.dram_tensor("x_out", [D, NTOK], F32, kind="ExternalOutput").ap()
    for nm, shp in (("l1wq", [D, D]), ("l1wk", [D, D]), ("l1wv", [D, D]),
                    ("l1wo", [D, D]), ("l1bkv", [128, 2048]), ("l1bqo", [128, 16])):
        W[nm] = nc.dram_tensor(nm, shp, F32, kind="ExternalInput").ap()

    B = alloc_common(nc, st)
    PS = alloc_psum(nc, st)
    with nc.Block() as blk:
        p.blk = blk
        emit_prelude(p, B, W, xin)
        emit_l1_attn(p, B, W, PS, groups)
        emit_store(p, xout, B)
    st.close()
    return nc


def build_l0_attn_program(groups):
    nc = bass.Bass("TRN2", target_bir_lowering=False, debug=False)
    st = ExitStack()
    p = Prog(nc, st)
    W = declare_common_inputs(nc)
    xin = nc.dram_tensor("x_in", [D, NTOK], F32, kind="ExternalInput").ap()
    xout = nc.dram_tensor("x_out", [D, NTOK], F32, kind="ExternalOutput").ap()
    for nm, shp in (("l0wq", [D, D]), ("l0wk", [D, D]), ("l0wv", [D, D]),
                    ("l0wo", [D, D]), ("l0ctab", [128, 512]), ("l0stab", [128, 512]),
                    ("l0tri", [128, 128]), ("l0psel", [128, 128]),
                    ("l0kvmask", [128, 4])):
        W[nm] = nc.dram_tensor(nm, shp, F32, kind="ExternalInput").ap()

    B = alloc_common(nc, st)
    PS = alloc_psum(nc, st)
    with nc.Block() as blk:
        p.blk = blk
        emit_prelude(p, B, W, xin)
        emit_l0_attn(p, B, W, PS, groups)
        emit_store(p, xout, B)
    st.close()
    return nc


def _consts_array():
    c = np.zeros((128, 16), np.float32)
    c[:, 0] = 1.0
    c[:, 2:10] = np.tile(np.arange(8, dtype=np.float32), (128, 1))
    c[0, 10] = 1e-5
    return c


def _prep_weights(inputs):
    w = {}
    w["l0gate"] = np.ascontiguousarray(np.asarray(inputs["L0_ffn_gate_w"], np.float32).T)
    w["l0w1"] = np.ascontiguousarray(np.transpose(np.asarray(inputs["L0_ffn_w1"], np.float32), (0, 2, 1)))
    w["l0w3"] = np.ascontiguousarray(np.transpose(np.asarray(inputs["L0_ffn_w3"], np.float32), (0, 2, 1)))
    w["l0w2"] = np.ascontiguousarray(np.transpose(np.asarray(inputs["L0_ffn_w2"], np.float32), (0, 2, 1)))
    w["consts"] = _consts_array()
    w["identf"] = np.eye(128, dtype=np.float32)
    w["onesrow"] = np.ones((1, 128), dtype=np.float32)
    es = np.zeros((8, 1024), np.float32)
    for e in range(8):
        es[e, e * 128:(e + 1) * 128] = 1.0
    w["esel"] = es
    w["l1gate"] = np.ascontiguousarray(np.asarray(inputs["L1_ffn_gate"], np.float32).T)
    w["l1corrtm"] = np.tile(np.asarray(inputs["L1_ffn_corr_bias"], np.float32)[None, :], (128, 1))
    w["l1w1"] = np.ascontiguousarray(np.transpose(np.asarray(inputs["L1_ffn_w1"], np.float32), (0, 2, 1)))
    w["l1w2"] = np.ascontiguousarray(np.transpose(np.asarray(inputs["L1_ffn_w2"], np.float32), (0, 2, 1)))
    w["l1fc1"] = np.ascontiguousarray(np.asarray(inputs["L1_ffn_fc1_w"], np.float32).T)
    w["l1fc2"] = np.ascontiguousarray(np.asarray(inputs["L1_ffn_fc2_w"], np.float32).T)
    w["l1up"] = np.ascontiguousarray(np.asarray(inputs["L1_ffn_up_w"], np.float32).T)
    w["l1dn"] = np.ascontiguousarray(np.asarray(inputs["L1_ffn_down_w"], np.float32).T)
    wqkv = np.asarray(inputs["L1_attn_w_qkv"], np.float32)
    wp = wqkv.reshape(3, 128, 8, 1024).transpose(0, 2, 1, 3).reshape(3, 1024, 1024)
    w["l1wq"] = np.ascontiguousarray(wp[0].T)
    w["l1wk"] = np.ascontiguousarray(wp[1].T)
    w["l1wv"] = np.ascontiguousarray(wp[2].T)
    w["l1wo"] = np.ascontiguousarray(np.asarray(inputs["L1_attn_w_o"], np.float32).T)
    bp = np.asarray(inputs["L1_attn_b_qkv"], np.float32).reshape(3, 128, 8).transpose(0, 2, 1).reshape(3, 1024)
    bkv = np.zeros((128, 2048), np.float32)
    bkv[:, 0:1024] = bp[1][None, :]
    bkv[:, 1024:2048] = bp[2][None, :]
    w["l1bkv"] = bkv
    bqo = np.zeros((128, 16), np.float32)
    bqo[:, 0:8] = bp[0].reshape(8, 128).T
    bqo[:, 8:16] = np.asarray(inputs["L1_attn_b_o"], np.float32).reshape(8, 128).T
    w["l1bqo"] = bqo
    w["l0wq"] = np.ascontiguousarray(np.asarray(inputs["L0_attn_Wq"], np.float32).T)
    w["l0wk"] = np.ascontiguousarray(np.asarray(inputs["L0_attn_Wk"], np.float32).T)
    w["l0wv"] = np.ascontiguousarray(np.asarray(inputs["L0_attn_Wv"], np.float32).T)
    w["l0wo"] = np.ascontiguousarray(np.asarray(inputs["L0_attn_Wo"], np.float32).T)
    tri = np.tril(np.ones((128, 128), np.float32)).T
    w["l0tri"] = np.ascontiguousarray(tri)
    psl = np.zeros((128, 128), np.float32)
    psl[0, 0:64] = 1.0
    psl[64, 64:128] = 1.0
    w["l0psel"] = psl
    return w


def _core_tables(core):
    pos = np.arange((core % 4) * NTOK, (core % 4) * NTOK + NTOK, dtype=np.float32)
    inv = (np.float32(10000.0) ** (-np.arange(32, dtype=np.float32) / np.float32(32.0))).astype(np.float32)
    ang = (pos[:, None] * inv[None, :]).astype(np.float32)
    ct = np.cos(ang.astype(np.float64)).astype(np.float32).reshape(16, 128, 32).transpose(1, 0, 2).reshape(128, 512)
    st = np.sin(ang.astype(np.float64)).astype(np.float32).reshape(16, 128, 32).transpose(1, 0, 2).reshape(128, 512)
    m = np.zeros((128, 4), np.float32)
    m[:, 0:core % 4] = 1.0
    return {"l0ctab": np.ascontiguousarray(ct), "l0stab": np.ascontiguousarray(st),
            "l0kvmask": m}


def declare_all_weights(nc):
    W = declare_common_inputs(nc)
    W["l0gate"] = nc.dram_tensor("l0gate", [D, 8], F32, kind="ExternalInput").ap()
    W["l0w1"] = nc.dram_tensor("l0w1", [8, D, 2048], F32, kind="ExternalInput").ap()
    W["l0w3"] = nc.dram_tensor("l0w3", [8, D, 2048], F32, kind="ExternalInput").ap()
    W["l0w2"] = nc.dram_tensor("l0w2", [8, 2048, D], F32, kind="ExternalInput").ap()
    for nm, shp in (("l0wq", [D, D]), ("l0wk", [D, D]), ("l0wv", [D, D]),
                    ("l0wo", [D, D]), ("l0ctab", [128, 512]), ("l0stab", [128, 512]),
                    ("l0tri", [128, 128]), ("l0psel", [128, 128]),
                    ("l0kvmask", [128, 4])):
        W[nm] = nc.dram_tensor(nm, shp, F32, kind="ExternalInput").ap()
    W["l1gate"] = nc.dram_tensor("l1gate", [D, 8], F32, kind="ExternalInput").ap()
    W["l1corrtm"] = nc.dram_tensor("l1corrtm", [128, 8], F32, kind="ExternalInput").ap()
    W["l1w1"] = nc.dram_tensor("l1w1", [8, 512, 512], F32, kind="ExternalInput").ap()
    W["l1w2"] = nc.dram_tensor("l1w2", [8, 512, 512], F32, kind="ExternalInput").ap()
    W["l1fc1"] = nc.dram_tensor("l1fc1", [D, 512], F32, kind="ExternalInput").ap()
    W["l1fc2"] = nc.dram_tensor("l1fc2", [512, D], F32, kind="ExternalInput").ap()
    W["l1up"] = nc.dram_tensor("l1up", [D, 4096], F32, kind="ExternalInput").ap()
    W["l1dn"] = nc.dram_tensor("l1dn", [4096, D], F32, kind="ExternalInput").ap()
    for nm, shp in (("l1wq", [D, D]), ("l1wk", [D, D]), ("l1wv", [D, D]),
                    ("l1wo", [D, D]), ("l1bkv", [128, 2048]), ("l1bqo", [128, 16])):
        W[nm] = nc.dram_tensor(nm, shp, F32, kind="ExternalInput").ap()
    return W


def build_fused_program(groups):
    nc = bass.Bass("TRN2", target_bir_lowering=False, debug=False)
    st = ExitStack()
    p = Prog(nc, st)
    W = declare_all_weights(nc)
    xin = nc.dram_tensor("x_in", [D, NTOK], F32, kind="ExternalInput").ap()
    xout = nc.dram_tensor("x_out", [D, NTOK], F32, kind="ExternalOutput").ap()
    B = alloc_common(nc, st)
    PS = alloc_psum(nc, st)
    with nc.Block() as blk:
        p.blk = blk
        emit_prelude(p, B, W, xin)

        def c_corr(g):
            p.dgroup(g, "d1")
            B["tok_corr"][0] = p.dma(g, "d1", B["CORRTM"], W["l1corrtm"])
        p.on_g(c_corr)

        emit_l0_ffn(p, B, W, PS)
        emit_l0_attn(p, B, W, PS, groups)
        emit_l0_ffn(p, B, W, PS)
        emit_l1_ffn(p, B, W, PS)
        emit_l1_attn(p, B, W, PS, groups)
        emit_l1_ffn(p, B, W, PS)
        emit_store(p, xout, B)
    st.close()
    return nc


def _input_names(nc):
    names = set()
    for f in nc.m.functions:
        for alloc in f.allocations:
            if getattr(alloc, "kind", None) == "ExternalInput":
                for m in alloc.memorylocations or []:
                    names.add(m.name)
    return names


def kernel(**inputs):
    x = np.asarray(inputs["L0_attn_x"], np.float32).reshape(2 * 8192, D)
    shards = [np.ascontiguousarray(x[c * NTOK:(c + 1) * NTOK].T) for c in range(NCORE)]
    w = _prep_weights(inputs)
    tables = [_core_tables(c) for c in range(NCORE)]
    groups = [[0, 1, 2, 3], [4, 5, 6, 7]]

    nc = build_fused_program(groups)
    names = _input_names(nc)
    in_maps = []
    for c in range(NCORE):
        full = {"x_in": shards[c], **w, **tables[c]}
        in_maps.append({k: v for k, v in full.items() if k in names})
    res = run_bass_kernel_spmd(nc, in_maps, core_ids=list(range(NCORE)))
    cur = [res.results[c]["x_out"] for c in range(NCORE)]
    out = np.concatenate([cur[c].T for c in range(NCORE)], axis=0)
    return np.ascontiguousarray(out.reshape(2, 8192, D))


# version 09 (final)
# speedup vs baseline: 1.1228x; 1.1228x over previous
import numpy as np
import concourse.bass as bass
import concourse.mybir as mybir
from concourse.bass_utils import run_bass_kernel_spmd
from contextlib import ExitStack

F32 = mybir.dt.float32
BF16 = mybir.dt.bfloat16
AF = mybir.ActivationFunctionType
ALU = mybir.AluOpType
AX = mybir.AxisListType

NCORE = 8
NTOK = 2048
D = 1024
NDT = 8
CH = 128
NCH = NTOK // CH
BIG = 1e9


class Prog:
    NDSLOT = 6

    def __init__(self, nc, stack):
        self.nc = nc
        self.st = stack
        self.cnt = {"v": 0, "s": 0, "t": 0, "g": 0}
        self.sem = {k: stack.enter_context(nc.semaphore(f"sem_{k}"))
                    for k in self.cnt}
        for i in range(self.NDSLOT):
            k = f"d{i}"
            self.sem[k] = stack.enter_context(nc.semaphore(f"sem_{k}"))
            self.cnt[k] = 0
        self.dslot = 0
        self.ccsems = []
        self.blk = None

    def dma_slot(self):
        k = f"d{3 + (self.dslot % 2)}"
        self.dslot += 1
        return k

    def v(self, v, fn):
        v.wait_ge(self.sem["v"], self.cnt["v"])
        fn().then_inc(self.sem["v"], 1)
        self.cnt["v"] += 1
        return self.cnt["v"]

    def s(self, s, fn):
        s.wait_ge(self.sem["s"], self.cnt["s"])
        fn().then_inc(self.sem["s"], 1)
        self.cnt["s"] += 1
        return self.cnt["s"]

    def g(self, g, fn):
        g.wait_ge(self.sem["g"], self.cnt["g"])
        fn().then_inc(self.sem["g"], 1)
        self.cnt["g"] += 1
        return self.cnt["g"]

    def t(self, fn):
        fn().then_inc(self.sem["t"], 1)
        self.cnt["t"] += 1
        return self.cnt["t"]

    def dgroup(self, q, slot):
        if self.cnt[slot] > 0:
            q.wait_ge(self.sem[slot], self.cnt[slot])

    def dma(self, q, slot, out, in_):
        q.dma_start(out=out, in_=in_).then_inc(self.sem[slot], 16)
        self.cnt[slot] += 16
        return (slot, self.cnt[slot])

    def wait(self, eng, key, tok=None):
        if tok is None:
            key, tok = key
        if tok > 0:
            eng.wait_ge(self.sem[key], tok)

    def on_v(self, fn):
        self.blk.vector(fn)

    def on_s(self, fn):
        self.blk.scalar(fn)

    def on_t(self, fn):
        self.blk.tensor(fn)

    def on_g(self, fn):
        self.blk.gpsimd(fn)

    def on_q(self, fn):
        self.blk.sync(fn)


def emit_ln(p, B, PS):
    X, XN = B["X"], B["XN"]
    sums = B["FT"][0:1, 0:NTOK]
    sumsq = B["ROWF"][:, 0:NTOK]
    mean = sums
    var = sumsq
    std = sumsq
    rr = sumsq
    RBC = B["PC"][:, 3072:5120]
    MBC = B["PC"][:, 5120:7168]
    SQ = B["PC"][:, 0:512]
    TMP = B["FT"][:, 0:NTOK]

    ps_s1 = PS["r1"]
    ps_s2 = PS["r2"]
    ps_bc = PS["a"]
    onesc = B["onesc"]
    onesr = B["onesr"]
    tok_x = B["tok_x"][0]

    cast_toks = []

    def c_cast(s):
        p.wait(s, tok_x)
        p.wait(s, "v", p.cnt["v"])
        for d in range(NDT):
            cast_toks.append(p.s(s, lambda d=d: s.copy(
                XN[:, d * NTOK:(d + 1) * NTOK], X[:, d * NTOK:(d + 1) * NTOK])))
    p.on_s(c_cast)
    tok_cast = p.cnt["s"]

    for nch in range(4):
        lo = nch * 512

        def c_sum(t, lo=lo):
            def f():
                p.wait(t, "s", tok_cast)
                p.wait(t, B["tok_consts"][0])
                p.wait(t, "v", p.cnt["v"])
                last = None
                for d in range(NDT):
                    last = t.matmul(ps_s1[:, :], onesc,
                                    XN[:, d * NTOK + lo:d * NTOK + lo + 512],
                                    start=(d == 0), stop=(d == NDT - 1))
                return last
            p.t(f)
        p.on_t(c_sum)
        tok_tsum = p.cnt["t"]

        for d in range(NDT):
            def c_sq(v, d=d, lo=lo):
                p.wait(v, "t", p.cnt["t"])
                p.wait(v, tok_x)
                p.v(v, lambda: v.tensor_mul(
                    SQ, X[:, d * NTOK + lo:d * NTOK + lo + 512],
                    X[:, d * NTOK + lo:d * NTOK + lo + 512]))
            p.on_v(c_sq)
            tok_v = p.cnt["v"]

            def c_mmsq(t, d=d, tok_v=tok_v):
                def f():
                    p.wait(t, "v", tok_v)
                    return t.matmul(ps_s2[:, :], onesc, SQ,
                                    start=(d == 0), stop=(d == NDT - 1))
                p.t(f)
            p.on_t(c_mmsq)
        tok_tsq = p.cnt["t"]

        def c_evac(v, lo=lo, a=tok_tsum, b=tok_tsq):
            p.wait(v, "t", b)
            p.v(v, lambda: v.tensor_copy(sums[:, lo:lo + 512], ps_s1[:, :]))
            p.v(v, lambda: v.tensor_copy(sumsq[:, lo:lo + 512], ps_s2[:, :]))
        p.on_v(c_evac)

    msq = B["DENF"][0:1, 0:256]

    def c_rows(v):
        p.v(v, lambda: v.tensor_scalar_mul(mean, sums, 1.0 / D))
        for i in range(8):
            sl = slice(i * 256, (i + 1) * 256)
            p.v(v, lambda sl=sl: v.tensor_mul(msq, mean[:, sl], mean[:, sl]))
            p.v(v, lambda sl=sl: v.scalar_tensor_tensor(
                var[:, sl], sumsq[:, sl], 1.0 / D, msq, ALU.mult, ALU.subtract))
    p.on_v(c_rows)
    tok_var = p.cnt["v"]

    def c_std(s):
        p.wait(s, "v", tok_var)
        p.s(s, lambda: s.activation(std, var, AF.Sqrt, bias=B["EPS"]))
    p.on_s(c_std)
    tok_std = p.cnt["s"]

    def c_recip(v):
        p.wait(v, "s", tok_std)
        p.v(v, lambda: v.reciprocal(rr, std))
    p.on_v(c_recip)

    onesrf = B["onesrf"]
    for row, dst in ((rr, RBC), (mean, MBC)):
        for nch in range(4):
            lo = nch * 512

            def c_bc(t, row=row, lo=lo):
                def f():
                    p.wait(t, "v", p.cnt["v"])
                    return t.matmul(ps_bc[:, :], onesrf, row[:, lo:lo + 512],
                                    start=True, stop=True)
                p.t(f)
            p.on_t(c_bc)
            tok_t = p.cnt["t"]

            def c_bev(v, dst=dst, lo=lo, tok_t=tok_t):
                p.wait(v, "t", tok_t)
                p.v(v, lambda: v.tensor_copy(dst[:, lo:lo + 512], ps_bc[:, :]))
            p.on_v(c_bev)

    def c_mbc(v):
        p.v(v, lambda: v.tensor_mul(MBC, MBC, RBC))
    p.on_v(c_mbc)

    def c_apply(v):
        for d in range(NDT):
            sl = slice(d * NTOK, (d + 1) * NTOK)
            p.v(v, lambda sl=sl: v.tensor_mul(TMP, X[:, sl], RBC))
            p.v(v, lambda sl=sl: v.tensor_sub(XN[:, sl], TMP, MBC))
    p.on_v(c_apply)
    return p.cnt["v"]


def emit_top2(p, v, SC, IO, W1O, W2O, rt):
    r0, r1, m1, m2 = rt["r0"], rt["r1"], rt["m1"], rt["m2"]
    w0, w1 = rt["w0"], rt["w1"]
    p.v(v, lambda: v.reduce_max(r0, SC, AX.X))
    p.v(v, lambda: v.tensor_scalar(w0, SC, r0, None, ALU.is_equal))
    p.v(v, lambda: v.tensor_scalar(w1, w0, -BIG, BIG, ALU.mult, ALU.add))
    p.v(v, lambda: v.scalar_tensor_tensor(w0, IO, 1.0, w0, ALU.mult, ALU.mult))
    p.v(v, lambda: v.tensor_add(w1, w1, w0))
    p.v(v, lambda: v.tensor_reduce(r1, w1, AX.X, ALU.min))
    p.v(v, lambda: v.tensor_scalar(W1O, IO, r1, None, ALU.is_equal))
    p.v(v, lambda: v.tensor_mul(w0, W1O, SC))
    p.v(v, lambda: v.tensor_reduce(m1, w0, AX.X, ALU.add))
    p.v(v, lambda: v.scalar_tensor_tensor(w1, W1O, -BIG, SC, ALU.mult, ALU.add))
    p.v(v, lambda: v.reduce_max(r0, w1, AX.X))
    p.v(v, lambda: v.tensor_scalar(w0, w1, r0, None, ALU.is_equal))
    p.v(v, lambda: v.tensor_scalar(w1, w0, -BIG, BIG, ALU.mult, ALU.add))
    p.v(v, lambda: v.scalar_tensor_tensor(w0, IO, 1.0, w0, ALU.mult, ALU.mult))
    p.v(v, lambda: v.tensor_add(w1, w1, w0))
    p.v(v, lambda: v.tensor_reduce(r1, w1, AX.X, ALU.min))
    p.v(v, lambda: v.tensor_scalar(W2O, IO, r1, None, ALU.is_equal))
    p.v(v, lambda: v.tensor_mul(w0, W2O, SC))
    p.v(v, lambda: v.tensor_reduce(m2, w0, AX.X, ALU.add))
    return m1, m2


def emit_gate_scores(p, B, PS, gateT, tok_gate, xn_ready, SCEM):
    XN = B["XN"]
    ps_sc = PS["r8"]

    for nch in range(4):
        lo = nch * 512

        def c_mm(t, lo=lo):
            def f():
                p.wait(t, tok_gate)
                p.wait(t, "v", max(xn_ready, p.cnt["v"]))
                last = None
                for d in range(NDT):
                    last = t.matmul(ps_sc[:, :], gateT[:, d * 8:(d + 1) * 8],
                                    XN[:, d * NTOK + lo:d * NTOK + lo + 512],
                                    start=(d == 0), stop=(d == NDT - 1))
                return last
            p.t(f)
        p.on_t(c_mm)
        tok_t = p.cnt["t"]

        def c_ev(v, lo=lo, tok_t=tok_t):
            p.wait(v, "t", tok_t)
            p.v(v, lambda: v.tensor_copy(SCEM[:, lo:lo + 512], ps_sc[:, :]))
        p.on_v(c_ev)
    return p.cnt["v"]


def emit_l0_ffn(p, B, W, PS):
    X, XN = B["X"], B["XN"]
    PA, PB, PC = B["PA"], B["PB"], B["PC"]
    XT = B["XTRA"]
    RT = B["RT"]
    IO8 = B["iota8"]
    ident = B["ident"]
    onesr = B["onesr"]

    xn_ready = emit_ln(p, B, PS)

    gateT = B["WG"]

    tok_gate_box = [None]

    def c_gl(g):
        p.dgroup(g, "d2")
        tok_gate_box[0] = p.dma(g, "d2",
                                gateT[:, :].rearrange("p (kt e) -> p kt e", e=8),
                                W["l0gate"].rearrange("(kt p) e -> p kt e", p=128))
    p.on_g(c_gl)
    tok_gate = tok_gate_box[0]

    SCEM = B["SC_EM"]
    tok_scores = emit_gate_scores(p, B, PS, gateT, tok_gate, xn_ready, SCEM)

    ps_trf = PS["o4"][:, 0:128]

    SCtm = RT[:, 0:8]
    W1O = RT[:, 8:16]
    W2O = RT[:, 16:24]
    WV = RT[:, 24:32]
    rt = {"r0": RT[:, 32:33], "r1": RT[:, 33:34],
          "m1": RT[:, 34:35], "m2": RT[:, 35:36],
          "w0": RT[:, 40:48], "w1": RT[:, 48:56]}
    d12 = RT[:, 36:37]
    sgm = RT[:, 37:38]
    wvf = B["WVF"]
    identf = B["identf"]
    wr_bf = B["WRBF"]

    def c_scbf(v):
        p.v(v, lambda: v.memset(wvf, 0.0))
    p.on_v(c_scbf)
    tok_scbf = p.cnt["v"]

    for ch in range(NCH):
        lo = ch * CH

        def c_tr(t, lo=lo):
            def f():
                p.wait(t, "v", max(tok_scores, tok_scbf, p.cnt["v"]))
                p.wait(t, B["tok_consts"][0])
                return t.transpose(ps_trf[:, 0:8], SCEM[:, lo:lo + CH],
                                   identf[0:8, 0:8])
            p.t(f)
        p.on_t(c_tr)
        tok_t = p.cnt["t"]

        def c_route(v, tok_t=tok_t):
            p.wait(v, "t", tok_t)
            p.wait(v, B["tok_consts"][0])
            p.v(v, lambda: v.tensor_copy(SCtm, ps_trf[:, 0:8]))
            m1, m2 = emit_top2(p, v, SCtm, IO8, W1O, W2O, rt)
            p.v(v, lambda: v.tensor_sub(d12, m1, m2))
        p.on_v(c_route)
        tok_v = p.cnt["v"]

        def c_sig(s, tok_v=tok_v):
            p.wait(s, "v", tok_v)
            p.s(s, lambda: s.activation(sgm, d12, AF.Sigmoid))
        p.on_s(c_sig)
        tok_s = p.cnt["s"]

        def c_wv(v, tok_s=tok_s):
            p.wait(v, "s", tok_s)
            p.v(v, lambda: v.tensor_scalar(WV, W1O, sgm, None, ALU.mult))
            p.v(v, lambda: v.tensor_scalar(rt["w0"], W2O, sgm, None, ALU.mult))
            p.v(v, lambda: v.tensor_sub(rt["w0"], W2O, rt["w0"]))
            p.v(v, lambda: v.tensor_add(WV, WV, rt["w0"]))
            p.v(v, lambda: v.tensor_copy(wvf[:, 0:8], WV))
        p.on_v(c_wv)
        tok_v2 = p.cnt["v"]

        def c_trb(t, tok_v2=tok_v2):
            def f():
                p.wait(t, "v", max(tok_v2, p.cnt["v"]))
                return t.transpose(ps_trf[:, :], wvf, identf)
            p.t(f)
        p.on_t(c_trb)
        tok_t2 = p.cnt["t"]

        def c_wrow(v, lo=lo, tok_t2=tok_t2):
            p.wait(v, "t", tok_t2)
            p.v(v, lambda: v.tensor_copy(wr_bf[:, lo:lo + CH], ps_trf[0:8, 0:CH]))
        p.on_v(c_wrow)
    tok_wrbf = p.cnt["v"]

    acc = PB
    wbc = PC[:, 0:NTOK]
    w1s = PA[:, 0:4096]
    w3s = PA[:, 4096:8192]
    w2s = PA[:, 8192:12288]
    hqs = [PC[:, 3072:5120], XT[:, 0:2048]]
    PAIRMODE = 1
    if PAIRMODE:
        pairs = [(PS["a"], PS["b"], PC[:, 2048:2560], PC[:, 2560:3072]),
                 (PS["c"], PS["d"], PC[:, 5120:5632], PC[:, 5632:6144])]
    else:
        pairs = [(PS["a"], PS["b"], PC[:, 2048:2560], PC[:, 2560:3072])] * 2
    ps_o = PS["o4"]
    ps_bcb = PS["bc"]

    fidx = [0]
    cidx = [0]
    h_tok = {0: 0, 1: 0}
    tok_p1_prev = [p.cnt["t"]]
    tok_p2_par = {0: p.cnt["t"], 1: p.cnt["t"]}
    tok_p2_prev = [p.cnt["t"]]
    acc_tok = [p.cnt["v"]]
    ld13 = [None]
    ld2t = [None]

    first = [True]
    for e in range(8):
        for nch in range(4):
            lo = nch * 512

            def c_bc(t, e=e, lo=lo):
                def f():
                    p.wait(t, "v", max(tok_wrbf, p.cnt["v"]))
                    p.wait(t, B["tok_consts"][0])
                    return t.matmul(ps_bcb[:, :], B["ESEL"][:, e * 128:(e + 1) * 128],
                                    wr_bf[:, lo:lo + 512],
                                    start=True, stop=True)
                p.t(f)
            p.on_t(c_bc)
            tok_t = p.cnt["t"]

            def c_bev(v, lo=lo, tok_t=tok_t):
                p.wait(v, "t", tok_t)
                p.v(v, lambda: v.tensor_copy(wbc[:, lo:lo + 512], ps_bcb[:, :]))
            p.on_v(c_bev)
        tok_wbc = p.cnt["v"]

        for q in range(4):
            def c_load13(g, e=e, q=q, pw=tok_p1_prev[0]):
                p.wait(g, "t", pw)
                slot = p.dma_slot()
                p.dgroup(g, slot)
                p.dma(g, slot, w1s.rearrange("p (kt f) -> p kt f", f=512),
                      W["l0w1"][e, :, q * 512:(q + 1) * 512]
                      .rearrange("(kt p) f -> p kt f", p=128))
                ld13[0] = p.dma(g, slot, w3s.rearrange("p (kt f) -> p kt f", f=512),
                                W["l0w3"][e, :, q * 512:(q + 1) * 512]
                                .rearrange("(kt p) f -> p kt f", p=128))
            p.on_g(c_load13)

            def c_load2(g, e=e, q=q, pw=tok_p2_prev[0]):
                p.wait(g, "t", pw)
                slot = p.dma_slot()
                p.dgroup(g, slot)
                ld2t[0] = p.dma(g, slot, w2s.rearrange("p (ft d) -> p ft d", d=1024),
                                W["l0w2"][e, q * 512:(q + 1) * 512, :]
                                .rearrange("(ft p) d -> p ft d", p=128))
            p.on_g(c_load2)

            pending = [None]

            def emit_phase2(info):
                if info is None:
                    return
                lo, hq, hlast = info
                for dh in range(2):
                    def c_o(t, dh=dh, h=hlast, lw=ld2t[0], aw=acc_tok[0], hq=hq):
                        def f():
                            p.wait(t, lw)
                            p.wait(t, "v", max(h, aw))
                            last = None
                            for f4 in range(4):
                                for dt in range(4):
                                    last = t.matmul(
                                        ps_o[:, dt * 512:(dt + 1) * 512],
                                        w2s[:, f4 * 1024 + dh * 512 + dt * 128:
                                            f4 * 1024 + dh * 512 + (dt + 1) * 128],
                                        hq[:, f4 * 512:(f4 + 1) * 512],
                                        start=(f4 == 0), stop=(f4 == 3))
                            return last
                        p.t(f)
                    p.on_t(c_o)
                    tok_o = p.cnt["t"]

                    def c_acc(v, dh=dh, lo=lo, tok_o=tok_o, fw=first[0]):
                        p.wait(v, "t", tok_o)
                        for dt in range(4):
                            d = dh * 4 + dt
                            dst = acc[:, d * NTOK + lo:d * NTOK + lo + 512]
                            src = ps_o[:, dt * 512:(dt + 1) * 512]
                            if fw:
                                p.v(v, lambda dst=dst, src=src: v.tensor_copy(dst, src))
                            else:
                                p.v(v, lambda dst=dst, src=src: v.tensor_add(dst, dst, src))
                    p.on_v(c_acc)
                    acc_tok[0] = p.cnt["v"]
                cp = (cidx[0] - 1) % 2
                tok_p2_prev[0] = p.cnt["t"]

            for c in range(4):
                lo = c * 512
                cpar = cidx[0] % 2
                cidx[0] += 1
                hq = hqs[cpar]
                for f in range(4):
                    par = fidx[0] % 2
                    fidx[0] += 1
                    pa, pb, t1, t2 = pairs[par]
                    hw_val = h_tok[par] if PAIRMODE else max(h_tok.values())
                    hw_val = max(hw_val, tok_wbc)

                    def c_ab(t, q=q, f=f, lo=lo, pa=pa, pb=pb,
                             hw=hw_val, lw=ld13[0]):
                        def fa():
                            p.wait(t, lw)
                            p.wait(t, "v", max(xn_ready, hw))
                            last = None
                            for d in range(NDT):
                                last = t.matmul(
                                    pa[:, :],
                                    w1s[:, d * 512 + f * 128:d * 512 + (f + 1) * 128],
                                    XN[:, d * NTOK + lo:d * NTOK + lo + 512],
                                    start=(d == 0), stop=(d == NDT - 1))
                            for d in range(NDT):
                                last = t.matmul(
                                    pb[:, :],
                                    w3s[:, d * 512 + f * 128:d * 512 + (f + 1) * 128],
                                    XN[:, d * NTOK + lo:d * NTOK + lo + 512],
                                    start=(d == 0), stop=(d == NDT - 1))
                            return last
                        p.t(fa)
                    p.on_t(c_ab)
                    tok_ab = p.cnt["t"]

                    def c_silu(sc, pa=pa, t1=t1, tok_ab=tok_ab, hw=hw_val):
                        p.wait(sc, "t", tok_ab)
                        p.wait(sc, "v", hw)
                        p.s(sc, lambda: sc.activation(t1, pa[:, :], AF.Silu))
                    p.on_s(c_silu)
                    tok_silu = p.cnt["s"]

                    def c_h(v, f=f, lo=lo, pb=pb, t1=t1, t2=t2, hq=hq,
                            tok_silu=tok_silu, p2w=tok_p2_par[cpar]):
                        p.wait(v, "s", tok_silu)
                        p.wait(v, "v", tok_wbc)
                        p.wait(v, "t", p2w)
                        p.v(v, lambda: v.tensor_mul(t2, pb[:, :], wbc[:, lo:lo + 512]))
                        p.v(v, lambda: v.tensor_mul(
                            hq[:, f * 512:(f + 1) * 512], t1, t2))
                    p.on_v(c_h)
                    h_tok[par] = p.cnt["v"]
                tok_p1_prev[0] = p.cnt["t"]
                hlast = p.cnt["v"]

                emit_phase2(pending[0])
                tok_p2_par[cpar ^ 1] = p.cnt["t"]
                pending[0] = (lo, hq, hlast)
            emit_phase2(pending[0])
            tok_p2_par[(cidx[0] - 1) % 2] = p.cnt["t"]
            if e == 0 and q == 0:
                first[0] = False

    def c_res(v):
        for d in range(NDT):
            sl = slice(d * NTOK, (d + 1) * NTOK)
            p.v(v, lambda sl=sl: v.scalar_tensor_tensor(
                X[:, sl], acc[:, sl], 0.5, X[:, sl], ALU.mult, ALU.add))
    p.on_v(c_res)




def emit_l1_ffn(p, B, W, PS):
    X, XN = B["X"], B["XN"]
    PA, PB, PC = B["PA"], B["PB"], B["PC"]
    RT = B["RT"]
    IO8 = B["iota8"]
    identf = B["identf"]
    ps_a, ps_b = PS["a"], PS["b"]
    ps_o = PS["o4"]

    xn_ready = emit_ln(p, B, PS)

    ups = PA[:, 0:4096]
    dns = PA[:, 4096:6144]
    h_all = PB
    t1 = PC[:, 2048:2560]

    for c in range(4):
        lo = c * 512
        for u8 in range(8):
            tok_w = [None]

            def c_ld(g, u8=u8):
                p.wait(g, "t", p.cnt["t"])
                slot = p.dma_slot()
                p.dgroup(g, slot)
                tok_w[0] = p.dma(
                    g, slot,
                    ups.rearrange("p (kt f) -> p kt f", f=512),
                    W["l1up"][:, u8 * 512:(u8 + 1) * 512]
                    .rearrange("(kt p) f -> p kt f", p=128))
            p.on_g(c_ld)

            for ut in range(4):
                def c_u(t, u8=u8, ut=ut, lo=lo, tw=tok_w[0]):
                    def f():
                        p.wait(t, tw)
                        p.wait(t, "v", max(xn_ready, p.cnt["v"]))
                        p.wait(t, "s", p.cnt["s"])
                        last = None
                        for d in range(NDT):
                            last = t.matmul(
                                ps_a[:, :],
                                ups[:, d * 512 + ut * 128:d * 512 + (ut + 1) * 128],
                                XN[:, d * NTOK + lo:d * NTOK + lo + 512],
                                start=(d == 0), stop=(d == NDT - 1))
                        return last
                    p.t(f)
                p.on_t(c_u)
                tok_u = p.cnt["t"]

                def c_r(s, tok_u=tok_u):
                    p.wait(s, "t", tok_u)
                    p.wait(s, "v", p.cnt["v"])
                    p.s(s, lambda: s.activation(t1, ps_a[:, :], AF.Relu))
                p.on_s(c_r)
                tok_r = p.cnt["s"]

                def c_sqh(v, u8=u8, ut=ut, tok_r=tok_r):
                    p.wait(v, "s", tok_r)
                    dst = h_all[:, (u8 * 4 + ut) * 512:(u8 * 4 + ut + 1) * 512]
                    p.v(v, lambda: v.tensor_mul(dst, t1, t1))
                p.on_v(c_sqh)
        tok_h = p.cnt["v"]

        for dh in range(2):
            for u8 in range(8):
                tok_w2 = [None]

                def c_ld2(g, u8=u8, dh=dh):
                    p.wait(g, "t", p.cnt["t"])
                    slot = p.dma_slot()
                    p.dgroup(g, slot)
                    tok_w2[0] = p.dma(
                        g, slot,
                        dns.rearrange("p (kt f) -> p kt f", f=512),
                        W["l1dn"][u8 * 512:(u8 + 1) * 512,
                                  dh * 512:(dh + 1) * 512]
                        .rearrange("(kt p) f -> p kt f", p=128))
                p.on_g(c_ld2)

                def c_d(t, u8=u8, dh=dh, tw=tok_w2[0], tok_h=tok_h):
                    def f():
                        p.wait(t, tw)
                        p.wait(t, "v", max(tok_h, p.cnt["v"]))
                        last = None
                        for ut in range(4):
                            for dt in range(4):
                                last = t.matmul(
                                    ps_o[:, dt * 512:(dt + 1) * 512],
                                    dns[:, ut * 512 + dt * 128:ut * 512 + (dt + 1) * 128],
                                    h_all[:, (u8 * 4 + ut) * 512:(u8 * 4 + ut + 1) * 512],
                                    start=(u8 == 0 and ut == 0),
                                    stop=(u8 == 7 and ut == 3))
                        return last
                    p.t(f)
                p.on_t(c_d)
            tok_d = p.cnt["t"]

            def c_add(v, dh=dh, lo=lo, tok_d=tok_d):
                p.wait(v, "t", tok_d)
                for dt in range(4):
                    d = dh * 4 + dt
                    sl = slice(d * NTOK + lo, d * NTOK + lo + 512)
                    p.v(v, lambda sl=sl, dt=dt: v.scalar_tensor_tensor(
                        X[:, sl], ps_o[:, dt * 512:(dt + 1) * 512], 0.5,
                        X[:, sl], ALU.mult, ALU.add))
            p.on_v(c_add)

    gateT = B["WG"]
    tok_gate_box = [None]

    def c_gl(g):
        p.dgroup(g, "d2")
        tok_gate_box[0] = p.dma(g, "d2",
                                gateT[:, :].rearrange("p (kt e) -> p kt e", e=8),
                                W["l1gate"].rearrange("(kt p) e -> p kt e", p=128))
    p.on_g(c_gl)
    tok_gate = tok_gate_box[0]

    SCEM = B["SC_EM"]
    tok_raw = emit_gate_scores(p, B, PS, gateT, tok_gate, xn_ready, SCEM)

    def c_sig_all(s):
        p.wait(s, "v", tok_raw)
        p.s(s, lambda: s.activation(SCEM, SCEM, AF.Sigmoid))
    p.on_s(c_sig_all)
    tok_sig = p.cnt["s"]

    ps_trf = PS["o4"][:, 0:128]
    wr_bf = B["WRBF"]
    SCtm = RT[:, 0:8]
    SFC = RT[:, 56:64]
    W1O = RT[:, 8:16]
    W2O = RT[:, 16:24]
    WV = RT[:, 24:32]
    rt = {"r0": RT[:, 32:33], "r1": RT[:, 33:34],
          "m1": RT[:, 34:35], "m2": RT[:, 35:36],
          "w0": RT[:, 40:48], "w1": RT[:, 48:56]}
    gs0 = RT[:, 36:37]
    gs1 = RT[:, 37:38]
    grp1 = RT[:, 38:39]
    den = RT[:, 39:40]
    CORRTM = B["CORRTM"]
    wvf = B["WVF"]

    def c_w0(v):
        p.v(v, lambda: v.memset(wvf, 0.0))
    p.on_v(c_w0)
    tok_w0 = p.cnt["v"]

    def half_top2(p, v, Gs, IOs, hot, w0, w1, r0, r1, msum):
        p.v(v, lambda: v.reduce_max(r0, Gs, AX.X))
        p.v(v, lambda: v.tensor_scalar(w0, Gs, r0, None, ALU.is_equal))
        p.v(v, lambda: v.tensor_scalar(w1, w0, -BIG, BIG, ALU.mult, ALU.add))
        p.v(v, lambda: v.scalar_tensor_tensor(w0, IOs, 1.0, w0, ALU.mult, ALU.mult))
        p.v(v, lambda: v.tensor_add(w1, w1, w0))
        p.v(v, lambda: v.tensor_reduce(r1, w1, AX.X, ALU.min))
        p.v(v, lambda: v.tensor_scalar(hot, IOs, r1, None, ALU.is_equal))
        p.v(v, lambda: v.scalar_tensor_tensor(w0, hot, -BIG, Gs, ALU.mult, ALU.add))
        p.v(v, lambda: v.reduce_max(r1, w0, AX.X))
        p.v(v, lambda: v.tensor_add(msum, r0, r1))

    for ch in range(NCH):
        lo = ch * CH

        def c_tr(t, lo=lo):
            def f():
                p.wait(t, "s", tok_sig)
                p.wait(t, "v", max(tok_w0, p.cnt["v"]))
                p.wait(t, B["tok_consts"][0])
                return t.transpose(ps_trf[:, 0:8], SCEM[:, lo:lo + CH],
                                   identf[0:8, 0:8])
            p.t(f)
        p.on_t(c_tr)
        tok_t = p.cnt["t"]

        def c_route(v, tok_t=tok_t):
            p.wait(v, "t", tok_t)
            p.wait(v, B["tok_consts"][0])
            p.wait(v, B["tok_corr"][0])
            p.v(v, lambda: v.tensor_copy(SCtm, ps_trf[:, 0:8]))
            p.v(v, lambda: v.tensor_add(SFC, SCtm, CORRTM))
            half_top2(p, v, SFC[:, 0:4], IO8[:, 0:4], rt["w0"][:, 0:4],
                      rt["w0"][:, 4:8], rt["w1"][:, 0:4], rt["r0"], rt["r1"], gs0)
            half_top2(p, v, SFC[:, 4:8], IO8[:, 0:4], rt["w0"][:, 0:4],
                      rt["w0"][:, 4:8], rt["w1"][:, 0:4], rt["r0"], rt["r1"], gs1)
            p.v(v, lambda: v.tensor_tensor(grp1, gs1, gs0, ALU.is_gt))
            p.v(v, lambda: v.memset(rt["w1"], 1.0))
            p.v(v, lambda: v.tensor_scalar(rt["w1"][:, 0:4], rt["w1"][:, 0:4],
                                           grp1, None, ALU.subtract))
            p.v(v, lambda: v.tensor_scalar(rt["w1"][:, 4:8], rt["w1"][:, 4:8],
                                           grp1, None, ALU.mult))
            p.v(v, lambda: v.tensor_mul(SFC, SFC, rt["w1"]))
            emit_top2(p, v, SFC, IO8, W1O, W2O, rt)
            p.v(v, lambda: v.tensor_mul(rt["w0"], W1O, SCtm))
            p.v(v, lambda: v.tensor_reduce(rt["m1"], rt["w0"], AX.X, ALU.add))
            p.v(v, lambda: v.tensor_mul(rt["w0"], W2O, SCtm))
            p.v(v, lambda: v.tensor_reduce(rt["m2"], rt["w0"], AX.X, ALU.add))
            p.v(v, lambda: v.tensor_add(den, rt["m1"], rt["m2"]))
            p.v(v, lambda: v.tensor_scalar_add(den, den, 1e-20))
            p.v(v, lambda: v.reciprocal(den, den))
            p.v(v, lambda: v.tensor_mul(rt["m1"], rt["m1"], den))
            p.v(v, lambda: v.tensor_mul(rt["m2"], rt["m2"], den))
            p.v(v, lambda: v.tensor_scalar(WV, W1O, rt["m1"], None, ALU.mult))
            p.v(v, lambda: v.tensor_scalar(rt["w0"], W2O, rt["m2"], None, ALU.mult))
            p.v(v, lambda: v.tensor_add(WV, WV, rt["w0"]))
            p.v(v, lambda: v.tensor_copy(wvf[:, 0:8], WV))
        p.on_v(c_route)
        tok_v2 = p.cnt["v"]

        def c_trb(t, tok_v2=tok_v2):
            def f():
                p.wait(t, "v", max(tok_v2, p.cnt["v"]))
                return t.transpose(ps_trf[:, :], wvf, identf)
            p.t(f)
        p.on_t(c_trb)
        tok_t2 = p.cnt["t"]

        def c_wrow(v, lo=lo, tok_t2=tok_t2):
            p.wait(v, "t", tok_t2)
            p.v(v, lambda: v.tensor_copy(wr_bf[:, lo:lo + CH], ps_trf[0:8, 0:CH]))
        p.on_v(c_wrow)
    tok_wrbf = p.cnt["v"]

    fc1s = PA[:, 0:4096]
    lat = PB[:, 0:8192]
    comb = PB[:, 8192:16384]
    tok_fc1 = [None]

    def c_lf(g):
        p.wait(g, "t", p.cnt["t"])
        slot = p.dma_slot()
        p.dgroup(g, slot)
        tok_fc1[0] = p.dma(g, slot,
                           fc1s.rearrange("p (kt f) -> p kt f", f=512),
                           W["l1fc1"].rearrange("(kt p) f -> p kt f", p=128))
    p.on_g(c_lf)

    for lt in range(4):
        for c in range(4):
            lo = c * 512

            def c_lat(t, lt=lt, lo=lo, tw=tok_fc1[0]):
                def f():
                    p.wait(t, tw)
                    p.wait(t, "v", max(xn_ready, p.cnt["v"]))
                    last = None
                    for d in range(NDT):
                        last = t.matmul(
                            ps_a[:, :],
                            fc1s[:, d * 512 + lt * 128:d * 512 + (lt + 1) * 128],
                            XN[:, d * NTOK + lo:d * NTOK + lo + 512],
                            start=(d == 0), stop=(d == NDT - 1))
                    return last
                p.t(f)
            p.on_t(c_lat)
            tok_t = p.cnt["t"]

            def c_ev(v, lt=lt, lo=lo, tok_t=tok_t):
                p.wait(v, "t", tok_t)
                p.v(v, lambda: v.tensor_copy(
                    lat[:, lt * NTOK + lo:lt * NTOK + lo + 512], ps_a[:, :]))
            p.on_v(c_ev)
    tok_lat = p.cnt["v"]

    wexp = [(PA[:, 0:2048], PA[:, 2048:4096]),
            (PA[:, 4096:6144], PA[:, 6144:8192])]
    hb = PA[:, 8192:10240]
    wbc = PC[:, 0:NTOK]
    t2 = PC[:, 3072:3584]
    ps_u = [ps_a, PS["c"]]
    t1_s = [PC[:, 2048:2560], PC[:, 2560:3072]]
    ps_bcb = PS["bc"]
    we_user = {0: p.cnt["t"], 1: p.cnt["t"]}
    we_ld = {}
    sqe_tok = {0: p.cnt["v"], 1: p.cnt["v"]}
    ho_tok = [p.cnt["t"]]
    comb_tok = [p.cnt["v"]]

    first = [True]
    for e in range(8):
        for nch in range(4):
            lo = nch * 512

            def c_bc(t, e=e, lo=lo):
                def f():
                    p.wait(t, "v", max(tok_wrbf, p.cnt["v"]))
                    p.wait(t, B["tok_consts"][0])
                    return t.matmul(ps_bcb[:, :], B["ESEL"][:, e * 128:(e + 1) * 128],
                                    wr_bf[:, lo:lo + 512], start=True, stop=True)
                p.t(f)
            p.on_t(c_bc)
            tok_t = p.cnt["t"]

            def c_bev(v, lo=lo, tok_t=tok_t):
                p.wait(v, "t", tok_t)
                p.v(v, lambda: v.tensor_copy(wbc[:, lo:lo + 512], ps_bcb[:, :]))
            p.on_v(c_bev)
        tok_wbc = p.cnt["v"]

        ep = e % 2
        w1s, w2s = wexp[ep]

        def c_lw(g, e=e, ep=ep, uw=we_user[ep]):
            p.wait(g, "t", uw if e >= 2 else p.cnt["t"])
            slot = p.dma_slot()
            p.dgroup(g, slot)
            p.dma(g, slot, w1s.rearrange("p (kt f) -> p kt f", f=512),
                  W["l1w1"][e].rearrange("(kt p) f -> p kt f", p=128))
            we_ld[ep] = p.dma(g, slot, w2s.rearrange("p (kt f) -> p kt f", f=512),
                              W["l1w2"][e].rearrange("(kt p) f -> p kt f", p=128))
        p.on_g(c_lw)

        for c in range(4):
            lo = c * 512
            for it in range(4):
                upar = it % 2

                def c_a(t, it=it, lo=lo, upar=upar, w1s=w1s,
                        tw=we_ld[ep], tok_lat=tok_lat,
                        sq=max(sqe_tok[upar], tok_wbc)):
                    def f():
                        p.wait(t, tw)
                        p.wait(t, "v", max(tok_lat, sq))
                        last = None
                        for ltk in range(4):
                            last = t.matmul(
                                ps_u[upar][:, :],
                                w1s[:, ltk * 512 + it * 128:ltk * 512 + (it + 1) * 128],
                                lat[:, ltk * NTOK + lo:ltk * NTOK + lo + 512],
                                start=(ltk == 0), stop=(ltk == 3))
                        return last
                    p.t(f)
                p.on_t(c_a)
                tok_a = p.cnt["t"]

                def c_r(s, upar=upar, tok_a=tok_a, sq=sqe_tok[upar]):
                    p.wait(s, "t", tok_a)
                    p.wait(s, "v", sq)
                    p.s(s, lambda: s.activation(t1_s[upar], ps_u[upar][:, :], AF.Relu))
                p.on_s(c_r)
                tok_r = p.cnt["s"]

                def c_h(v, it=it, lo=lo, upar=upar, tok_r=tok_r,
                        tok_wbc=tok_wbc, how=ho_tok[0]):
                    p.wait(v, "s", tok_r)
                    p.wait(v, "v", tok_wbc)
                    p.wait(v, "t", how)
                    p.v(v, lambda: v.tensor_mul(t2, t1_s[upar], wbc[:, lo:lo + 512]))
                    p.v(v, lambda: v.tensor_mul(
                        hb[:, it * 512:(it + 1) * 512], t1_s[upar], t2))
                p.on_v(c_h)
                sqe_tok[upar] = p.cnt["v"]
            tok_h = p.cnt["v"]

            def c_o(t, tok_h=tok_h, w2s=w2s, tw=we_ld[ep], cw=comb_tok[0]):
                def f():
                    p.wait(t, tw)
                    p.wait(t, "v", max(tok_h, cw))
                    last = None
                    for it in range(4):
                        for ltk in range(4):
                            last = t.matmul(
                                ps_o[:, ltk * 512:(ltk + 1) * 512],
                                w2s[:, it * 512 + ltk * 128:it * 512 + (ltk + 1) * 128],
                                hb[:, it * 512:(it + 1) * 512],
                                start=(it == 0), stop=(it == 3))
                    return last
                p.t(f)
            p.on_t(c_o)
            tok_o = p.cnt["t"]
            ho_tok[0] = tok_o

            def c_acc(v, lo=lo, tok_o=tok_o, fw=first[0]):
                p.wait(v, "t", tok_o)
                for ltk in range(4):
                    dst = comb[:, ltk * NTOK + lo:ltk * NTOK + lo + 512]
                    src = ps_o[:, ltk * 512:(ltk + 1) * 512]
                    if fw:
                        p.v(v, lambda dst=dst, src=src: v.tensor_copy(dst, src))
                    else:
                        p.v(v, lambda dst=dst, src=src: v.tensor_add(dst, dst, src))
            p.on_v(c_acc)
            comb_tok[0] = p.cnt["v"]
        we_user[ep] = p.cnt["t"]
        first[0] = False
    tok_comb = p.cnt["v"]

    fc2s = PA[:, 0:4096]
    tok_fc2 = [None]

    def c_lf2(g):
        p.wait(g, "t", p.cnt["t"])
        slot = p.dma_slot()
        p.dgroup(g, slot)
        tok_fc2[0] = p.dma(g, slot,
                           fc2s.rearrange("p (kt f) -> p kt f", f=1024),
                           W["l1fc2"].rearrange("(kt p) f -> p kt f", p=128))
    p.on_g(c_lf2)

    for c in range(4):
        lo = c * 512
        for dh in range(2):
            def c_f2(t, dh=dh, lo=lo, tw=tok_fc2[0], tok_comb=tok_comb):
                def f():
                    p.wait(t, tw)
                    p.wait(t, "v", max(tok_comb, p.cnt["v"]))
                    last = None
                    for ltk in range(4):
                        for dt in range(4):
                            last = t.matmul(
                                ps_o[:, dt * 512:(dt + 1) * 512],
                                fc2s[:, ltk * 1024 + dh * 512 + dt * 128:
                                     ltk * 1024 + dh * 512 + (dt + 1) * 128],
                                comb[:, ltk * NTOK + lo:ltk * NTOK + lo + 512],
                                start=(ltk == 0), stop=(ltk == 3))
                    return last
                p.t(f)
            p.on_t(c_f2)
            tok_t = p.cnt["t"]

            def c_add(v, dh=dh, lo=lo, tok_t=tok_t):
                p.wait(v, "t", tok_t)
                for dt in range(4):
                    d = dh * 4 + dt
                    sl = slice(d * NTOK + lo, d * NTOK + lo + 512)
                    p.v(v, lambda sl=sl, dt=dt: v.scalar_tensor_tensor(
                        X[:, sl], ps_o[:, dt * 512:(dt + 1) * 512], 0.5,
                        X[:, sl], ALU.mult, ALU.add))
            p.on_v(c_add)




def emit_l1_attn(p, B, W, PS, groups):
    X, XN = B["X"], B["XN"]
    def c_b0(g):
        p.dgroup(g, "d1")
        p.dma(g, "d1", B["BKV"], W["l1bkv"])
        B["tok_bkv"][0] = p.dma(g, "d1", B["BQO"], W["l1bqo"])
    p.on_g(c_b0)
    PA, PB, PC = B["PA"], B["PB"], B["PC"]
    FT = B["FT"]
    ps_a, ps_b, ps_o = PS["a"], PS["b"], PS["o4"]
    nc = p.nc

    xn_ready = emit_ln(p, B, PS)

    wq = PA[:, 0:8192]
    wk = PB[:, 0:8192]
    wv = PB[:, 8192:16384]
    kc = PC[:, 0:1024]
    vc = PC[:, 1024:2048]
    KtVb = PC[:, 2048:3072]
    qc = PC[:, 3072:4096]
    aoc = PC[:, 4096:5120]
    BKV = B["BKV"]
    BQ = B["BQO"][:, 0:8]
    BO = B["BQO"][:, 8:16]
    KVSB = FT[:, 0:1024]

    cc_in = nc.dram_tensor(f"cc_in_{p.cnt['t']}", [128, 1024], F32)
    cc_out = nc.dram_tensor(f"cc_out_{p.cnt['t']}", [128, 1024], F32)
    ccsem = p.st.enter_context(nc.semaphore(f"ccs_{p.cnt['t']}"))

    tok_wqkv = [None]

    def c_lw(g):
        p.wait(g, "t", p.cnt["t"])
        p.wait(g, "v", p.cnt["v"])
        slot = p.dma_slot()
        p.dgroup(g, slot)
        p.dma(g, slot, wq.rearrange("p (kt f) -> p kt f", f=1024),
              W["l1wq"].rearrange("(kt p) f -> p kt f", p=128))
        p.dma(g, slot, wk.rearrange("p (kt f) -> p kt f", f=1024),
              W["l1wk"].rearrange("(kt p) f -> p kt f", p=128))
        tok_wqkv[0] = p.dma(g, slot, wv.rearrange("p (kt f) -> p kt f", f=1024),
                            W["l1wv"].rearrange("(kt p) f -> p kt f", p=128))
    p.on_g(c_lw)
    tok_w = tok_wqkv[0]

    for ch in range(NCH):
        lo = ch * CH
        for half, pst in ((0, ps_a), (1, ps_b)):
            def c_k(t, lo=lo, half=half, pst=pst):
                def f():
                    p.wait(t, tok_w)
                    p.wait(t, B["tok_bkv"][0])
                    p.wait(t, "v", max(xn_ready, p.cnt["v"]))
                    last = None
                    for d in range(NDT):
                        last = t.matmul(
                            pst[:, :],
                            XN[:, d * NTOK + lo:d * NTOK + lo + CH],
                            wk[:, d * 1024 + half * 512:d * 1024 + (half + 1) * 512],
                            start=(d == 0), stop=(d == NDT - 1))
                    return last
                p.t(f)
            p.on_t(c_k)
            tok_k = p.cnt["t"]

            def c_ke(v, half=half, pst=pst, tok_k=tok_k):
                p.wait(v, "t", tok_k)
                p.v(v, lambda: v.tensor_add(
                    kc[:, half * 512:(half + 1) * 512], pst[:, :],
                    BKV[:, half * 512:(half + 1) * 512]))
            p.on_v(c_ke)

        for half, pst in ((0, ps_a), (1, ps_b)):
            def c_v(t, lo=lo, half=half, pst=pst):
                def f():
                    p.wait(t, "v", p.cnt["v"])
                    last = None
                    for d in range(NDT):
                        last = t.matmul(
                            pst[:, :],
                            XN[:, d * NTOK + lo:d * NTOK + lo + CH],
                            wv[:, d * 1024 + half * 512:d * 1024 + (half + 1) * 512],
                            start=(d == 0), stop=(d == NDT - 1))
                    return last
                p.t(f)
            p.on_t(c_v)
            tok_v = p.cnt["t"]

            def c_ve(v, half=half, pst=pst, tok_v=tok_v):
                p.wait(v, "t", tok_v)
                p.v(v, lambda: v.tensor_add(
                    vc[:, half * 512:(half + 1) * 512], pst[:, :],
                    BKV[:, 1024 + half * 512:1024 + (half + 1) * 512]))
            p.on_v(c_ve)
        tok_kv = p.cnt["v"]

        def c_ktv(t, ch=ch, tok_kv=tok_kv):
            def f():
                p.wait(t, "v", max(tok_kv, p.cnt["v"]))
                last = None
                for h in range(8):
                    last = t.matmul(
                        ps_o[:, h * 128:(h + 1) * 128],
                        kc[:, h * 128:(h + 1) * 128],
                        vc[:, h * 128:(h + 1) * 128],
                        start=(ch == 0 and h % 4 == 0),
                        stop=(ch == NCH - 1 and h % 4 == 3))
                return last
            p.t(f)
        p.on_t(c_ktv)

    tok_ktv = p.cnt["t"]

    def c_kve(v):
        p.wait(v, "t", tok_ktv)
        p.v(v, lambda: v.tensor_copy(KVSB, ps_o[:, 0:1024]))
    p.on_v(c_kve)
    tok_kvsb = p.cnt["v"]

    def c_cc(g):
        p.wait(g, "v", tok_kvsb)
        p.dgroup(g, "d2")
        t1 = p.dma(g, "d2", cc_in.ap(), KVSB)
        p.wait(g, t1)
        g.collective_compute(
            "AllReduce", ALU.add, replica_groups=groups,
            ins=[cc_in.ap()], outs=[cc_out.ap()]).then_inc(ccsem)
        g.wait_ge(ccsem, 1)
        t2 = p.dma(g, "d2", KVSB, cc_out.ap())
        p.wait(g, t2)
        p.g(g, lambda: g.memset(B["RT"][0:1, 63:64], 0.0))
    p.on_g(c_cc)
    tok_cc = p.cnt["g"]

    def c_kvb(v):
        p.wait(v, "g", tok_cc)
        p.v(v, lambda: v.tensor_copy(KtVb, KVSB))
    p.on_v(c_kvb)
    tok_kvb = p.cnt["v"]

    wo = PB[:, 0:8192]
    tok_wo_box = [None]

    def c_lwo(g):
        p.wait(g, "t", p.cnt["t"])
        slot = p.dma_slot()
        p.dgroup(g, slot)
        tok_wo_box[0] = p.dma(g, slot, wo.rearrange("p (kt f) -> p kt f", f=1024),
                              W["l1wo"].rearrange("(kt p) f -> p kt f", p=128))
    p.on_g(c_lwo)
    tok_wo = tok_wo_box[0]

    for ch in range(NCH):
        lo = ch * CH
        for dt in range(NDT):
            def c_q(t, dt=dt, lo=lo):
                def f():
                    p.wait(t, tok_w)
                    p.wait(t, "v", p.cnt["v"])
                    last = None
                    for d in range(NDT):
                        last = t.matmul(
                            ps_a[:, 0:128],
                            wq[:, d * 1024 + dt * 128:d * 1024 + (dt + 1) * 128],
                            XN[:, d * NTOK + lo:d * NTOK + lo + CH],
                            start=(d == 0), stop=(d == NDT - 1))
                    return last
                p.t(f)
            p.on_t(c_q)
            tok_q = p.cnt["t"]

            def c_qe(v, dt=dt, tok_q=tok_q):
                p.wait(v, "t", tok_q)
                p.wait(v, B["tok_bkv"][0])
                p.v(v, lambda: v.tensor_scalar(qc[:, dt * 128:(dt + 1) * 128],
                                               ps_a[:, 0:128], BQ[:, dt:dt + 1],
                                               None, ALU.add))
            p.on_v(c_qe)
        tok_qc = p.cnt["v"]

        for h in range(8):
            def c_oh(t, h=h, tok_qc=tok_qc):
                def f():
                    p.wait(t, "v", tok_qc)
                    p.wait(t, "v", max(tok_kvb, p.cnt["v"]))
                    return t.matmul(ps_b[:, 0:128],
                                    KtVb[:, h * 128:(h + 1) * 128],
                                    qc[:, h * 128:(h + 1) * 128],
                                    start=True, stop=True)
                p.t(f)
            p.on_t(c_oh)
            tok_oh = p.cnt["t"]

            def c_ohe(v, h=h, tok_oh=tok_oh):
                p.wait(v, "t", tok_oh)
                p.v(v, lambda: v.tensor_copy(aoc[:, h * 128:(h + 1) * 128],
                                             ps_b[:, 0:128]))
            p.on_v(c_ohe)
        tok_ao = p.cnt["v"]

        def c_wo(t, tok_ao=tok_ao):
            def f():
                p.wait(t, tok_wo)
                p.wait(t, "v", max(tok_ao, p.cnt["v"]))
                last = None
                for kt in range(8):
                    for dt in range(8):
                        last = t.matmul(
                            ps_o[:, 1024 + dt * 128:1024 + (dt + 1) * 128],
                            wo[:, kt * 1024 + dt * 128:kt * 1024 + (dt + 1) * 128],
                            aoc[:, kt * 128:(kt + 1) * 128],
                            start=(kt == 0 and dt % 4 == 0),
                            stop=(kt == 7 and dt % 4 == 3))
                return last
            p.t(f)
        p.on_t(c_wo)
        tok_t = p.cnt["t"]

        def c_res(v, lo=lo, tok_t=tok_t):
            p.wait(v, "t", tok_t)
            p.wait(v, B["tok_bkv"][0])
            for dt in range(NDT):
                sl = slice(dt * NTOK + lo, dt * NTOK + lo + CH)
                p.v(v, lambda sl=sl, dt=dt: v.scalar_tensor_tensor(
                    B["FT"][:, 1024 + dt * 0:1024 + CH] if False else X[:, sl],
                    ps_o[:, 1024 + dt * 128:1024 + (dt + 1) * 128],
                    BO[:, dt:dt + 1], X[:, sl], ALU.add, ALU.add))
        p.on_v(c_res)




def _rope_phi(p, B, src, scr, dst, ch, tok_pre):
    CTS, STS = B["CTS"], B["STS"]
    ct3 = CTS[:, ch * 32:(ch + 1) * 32].unsqueeze(1).broadcast_to((128, 16, 32))
    st3 = STS[:, ch * 32:(ch + 1) * 32].unsqueeze(1).broadcast_to((128, 16, 32))
    v4 = lambda ap: ap.rearrange("p (h t j) -> p h t j", t=2, j=32)
    x1 = v4(src)[:, :, 0, :]
    x2 = v4(src)[:, :, 1, :]
    s1 = v4(scr)[:, :, 0, :]
    s2 = v4(scr)[:, :, 1, :]

    def c_rope(v):
        p.wait(v, B["tok_ctab"][0])
        p.v(v, lambda: v.tensor_tensor(s1, x1, ct3, ALU.mult))
        p.v(v, lambda: v.tensor_tensor(s2, x2, ct3, ALU.mult))
        p.v(v, lambda: v.tensor_tensor(x2, x2, st3, ALU.mult))
        p.v(v, lambda: v.tensor_sub(s1, s1, x2))
        p.v(v, lambda: v.tensor_tensor(x1, x1, st3, ALU.mult))
        p.v(v, lambda: v.tensor_add(s2, s2, x1))
        p.v(v, lambda: v.tensor_scalar_min(src, scr, 0.0))
    p.on_v(c_rope)
    tok_m = p.cnt["v"]

    def c_exp(sc):
        p.wait(sc, "v", tok_m)
        p.s(sc, lambda: sc.activation(src, src, AF.Exp))
    p.on_s(c_exp)
    tok_e = p.cnt["s"]

    def c_phi(v):
        p.wait(v, "s", tok_e)
        p.v(v, lambda: v.scalar_tensor_tensor(dst, scr, 0.0, src, ALU.max, ALU.add))
    p.on_v(c_phi)
    return p.cnt["v"]


def emit_l0_attn(p, B, W, PS, groups):
    X, XN = B["X"], B["XN"]
    PA, PB, PC = B["PA"], B["PB"], B["PC"]
    FT = B["FT"]
    XT = B["XTRA"]
    ps_a, ps_b, ps_o, ps_tr = PS["a"], PS["b"], PS["o4"], PS["trp"]
    nc = p.nc
    onescb = B["onesc"]
    identb = B["ident"]

    xn_ready = emit_ln(p, B, PS)

    wq = PA[:, 0:8192]
    wk = PB[:, 0:8192]
    wv = PB[:, 8192:16384]
    wo_lo = PA[:, 8192:14336]
    wo_hi = XT[:, 2048:4096]

    kc = PC[:, 0:1024]
    vc = PC[:, 1024:2048]
    qc = PC[:, 2048:3072]
    kF = PC[:, 3072:4096]
    qF = PC[:, 4096:5120]
    Yc = PC[:, 5120:6144]
    ZB = PC[:, 6144:6656]
    AMB = XT[:, 1024:1280]
    KVB = XT[:, 1280:1408]
    RDB = XT[:, 1408:1536]
    DBC = XT[:, 1536:1664]
    PSL = XT[:, 1664:1792]
    TRIB = XT[:, 1792:1920]
    B["CTS"] = XT[:, 0:512]
    B["STS"] = XT[:, 512:1024]
    kf32 = FT[:, 0:1024]
    qf32 = FT[:, 1024:2048]
    KVSB = FT[:, 0:1024]
    STV = B["STV"]
    KVMASK = B["KVMASK"]

    nrank = len(groups[0])
    cc_in = nc.dram_tensor(f"cc0_in_{p.cnt['t']}", [128, 1024], F32)
    cc_ag = nc.dram_tensor(f"cc0_ag_{p.cnt['t']}", [nrank * 128, 1024], F32)
    ccsem = p.st.enter_context(nc.semaphore(f"ccs0_{p.cnt['t']}"))

    tok_w_box = [None]

    def c_lw(g):
        p.wait(g, "t", p.cnt["t"])
        p.wait(g, "v", p.cnt["v"])
        slot = p.dma_slot()
        p.dgroup(g, slot)
        p.dma(g, slot, wk.rearrange("p (kt f) -> p kt f", f=1024),
              W["l0wk"].rearrange("(kt p) f -> p kt f", p=128))
        p.dma(g, slot, wv.rearrange("p (kt f) -> p kt f", f=1024),
              W["l0wv"].rearrange("(kt p) f -> p kt f", p=128))
        tok_w_box[0] = p.dma(g, slot, wq.rearrange("p (kt f) -> p kt f", f=1024),
                             W["l0wq"].rearrange("(kt p) f -> p kt f", p=128))
    p.on_g(c_lw)
    tok_w = tok_w_box[0]

    def c_lc(g):
        p.dgroup(g, "d1")
        p.dma(g, "d1", B["CTS"], W["l0ctab"])
        p.dma(g, "d1", B["STS"], W["l0stab"])
        p.dma(g, "d1", TRIB, W["l0tri"])
        p.dma(g, "d1", PSL, W["l0psel"])
        B["tok_ctab"][0] = p.dma(g, "d1", KVMASK, W["l0kvmask"])
    p.on_g(c_lc)

    def c_z(v):
        p.v(v, lambda: v.memset(RDB, 0.0))
        p.v(v, lambda: v.memset(ZB, 0.0))
        p.v(v, lambda: v.memset(STV, 0.0))
    p.on_v(c_z)

    def proj_chunk(wmat, lo, dst_f32):
        for half, pst in ((0, ps_a), (1, ps_b)):
            def c_p(t, half=half, pst=pst, lo=lo, wmat=wmat):
                def f():
                    p.wait(t, tok_w)
                    p.wait(t, "v", max(xn_ready, p.cnt["v"]))
                    last = None
                    for d in range(NDT):
                        last = t.matmul(
                            pst[:, :],
                            XN[:, d * NTOK + lo:d * NTOK + lo + CH],
                            wmat[:, d * 1024 + half * 512:d * 1024 + (half + 1) * 512],
                            start=(d == 0), stop=(d == NDT - 1))
                    return last
                p.t(f)
            p.on_t(c_p)
            tok_t = p.cnt["t"]

            def c_e(v, half=half, pst=pst, tok_t=tok_t, dst=dst_f32):
                p.wait(v, "t", tok_t)
                p.v(v, lambda: v.tensor_copy(
                    dst[:, half * 512:(half + 1) * 512], pst[:, :]))
            p.on_v(c_e)

    def proj_v(lo):
        for half, pst in ((0, ps_a), (1, ps_b)):
            def c_pv(t, half=half, pst=pst, lo=lo):
                def f():
                    p.wait(t, "v", p.cnt["v"])
                    last = None
                    for d in range(NDT):
                        last = t.matmul(
                            pst[:, :],
                            XN[:, d * NTOK + lo:d * NTOK + lo + CH],
                            wv[:, d * 1024 + half * 512:d * 1024 + (half + 1) * 512],
                            start=(d == 0), stop=(d == NDT - 1))
                    return last
                p.t(f)
            p.on_t(c_pv)
            tok_t = p.cnt["t"]

            def c_ev(v, half=half, pst=pst, tok_t=tok_t):
                p.wait(v, "t", tok_t)
                p.v(v, lambda: v.tensor_copy(
                    vc[:, half * 512:(half + 1) * 512], pst[:, :]))
            p.on_v(c_ev)

    def state_update_group(ch, phase):
        tok_kv = p.cnt["v"]

        def c_upd(t, tok_kv=tok_kv):
            def f():
                p.wait(t, "v", max(tok_kv, p.cnt["v"]))
                p.wait(t, B["tok_consts"][0])
                last = None
                for bank in range(2):
                    t.matmul(ps_o[:, bank * 512:(bank + 1) * 512], identb, ZB,
                             start=True, stop=False)
                for h in range(16):
                    co = (h // 2 % 4) * 128 + (h // 8) * 512
                    po = (h % 2) * 64
                    t.matmul(ps_o[po:po + 64, co:co + 64],
                             kc[:, h * 64:(h + 1) * 64],
                             vc[:, h * 64:(h + 1) * 64],
                             start=False, stop=False)
                    last = t.matmul(ps_o[po:po + 64, co + 64:co + 65],
                                    kc[:, h * 64:(h + 1) * 64],
                                    onescb, start=False, stop=False)
                for bank in range(2):
                    last = t.matmul(ps_o[:, bank * 512:(bank + 1) * 512],
                                    identb, ZB, start=False, stop=True)
                return last
            p.t(f)
        p.on_t(c_upd)
        tok_u = p.cnt["t"]

        def c_acc(v, tok_u=tok_u):
            p.wait(v, "t", tok_u)
            p.v(v, lambda: v.tensor_add(STV[:, 0:512], STV[:, 0:512],
                                        ps_o[:, 0:512]))
            p.v(v, lambda: v.tensor_add(STV[:, 512:1024], STV[:, 512:1024],
                                        ps_o[:, 512:1024]))
        p.on_v(c_acc)

    for ch in range(NCH):
        lo = ch * CH
        proj_chunk(wk, lo, kf32)
        _rope_phi(p, B, kf32, qf32, kc, ch, None)
        proj_v(lo)
        state_update_group(ch, "A")

    tok_tot = p.cnt["v"]

    def c_cc(g):
        p.wait(g, "v", tok_tot)
        p.dgroup(g, "d2")
        t1 = p.dma(g, "d2", cc_in.ap(), STV)
        p.wait(g, t1)
        g.collective_compute(
            "AllGather", ALU.bypass, replica_groups=groups,
            ins=[cc_in.ap()], outs=[cc_ag.ap()]).then_inc(ccsem)
        g.wait_ge(ccsem, 1)
    p.on_g(c_cc)

    def c_st0(v):
        p.wait(v, "d2", p.cnt["d2"])
        v.wait_ge(ccsem, 1)
        p.v(v, lambda: v.memset(STV, 0.0))
    p.on_v(c_st0)
    for r in range(nrank):
        def c_ld(g, r=r):
            p.wait(g, "v", p.cnt["v"])
            g.wait_ge(ccsem, 1)
            p.dgroup(g, "d2")
            p.dma(g, "d2", KVSB, cc_ag.ap()[r * 128:(r + 1) * 128, :])
        p.on_g(c_ld)
        tok_r = ("d2", p.cnt["d2"])

        def c_fold(v, r=r, tok_r=tok_r):
            p.wait(v, tok_r)
            p.wait(v, B["tok_ctab"][0])
            p.v(v, lambda: v.scalar_tensor_tensor(
                STV, KVSB, KVMASK[:, r:r + 1], STV, ALU.mult, ALU.add))
        p.on_v(c_fold)
    tok_prefix = p.cnt["v"]

    tok_wo_box = [None]

    def c_lwo(g):
        p.wait(g, "t", p.cnt["t"])
        slot = p.dma_slot()
        p.dgroup(g, slot)
        p.dma(g, slot, wo_lo.rearrange("p (kt f) -> p kt f", f=1024),
              W["l0wo"][0:768, :].rearrange("(kt p) f -> p kt f", p=128))
        tok_wo_box[0] = p.dma(g, slot,
                              wo_hi.rearrange("p (kt f) -> p kt f", f=1024),
                              W["l0wo"][768:1024, :]
                              .rearrange("(kt p) f -> p kt f", p=128))
    p.on_g(c_lwo)
    tok_wo = tok_wo_box[0]

    for ch in range(NCH):
        lo = ch * CH
        proj_chunk(wk, lo, kf32)
        _rope_phi(p, B, kf32, qf32, kc, ch, None)
        proj_v(lo)
        proj_chunk(wq, lo, qf32)
        _rope_phi(p, B, qf32, kf32, qc, ch, None)
        tok_q = p.cnt["v"]

        for which, (srcb, dstb) in enumerate(((qc, qF), (kc, kF))):
            for pt in range(8):
                def c_tr(t, srcb=srcb, pt=pt):
                    def f():
                        p.wait(t, "v", max(tok_q, p.cnt["v"]))
                        return t.transpose(ps_tr[:, :],
                                           srcb[:, pt * 128:(pt + 1) * 128],
                                           identb)
                    p.t(f)
                p.on_t(c_tr)
                tok_t = p.cnt["t"]

                def c_te2(v, dstb=dstb, pt=pt, tok_t=tok_t):
                    p.wait(v, "t", tok_t)
                    p.v(v, lambda: v.tensor_copy(
                        dstb[:, pt * 128:(pt + 1) * 128], ps_tr[:, :]))
                p.on_v(c_te2)
        tok_tr = p.cnt["v"]

        for pr in range(8):
            co = (pr % 4) * 128 + (pr // 4) * 512

            def c_kvb(v, co=co, tok_tr=tok_tr):
                p.wait(v, "v", max(tok_tr, tok_prefix))
                p.v(v, lambda: v.tensor_copy(KVB[:, 0:65], STV[:, co:co + 65]))
            p.on_v(c_kvb)
            tok_kvb = p.cnt["v"]

            def c_A(t, pr=pr, tok_kvb=tok_kvb):
                def f():
                    p.wait(t, "v", max(tok_kvb, p.cnt["v"]))
                    t.matmul(ps_a[:, 0:128], kF[0:64, pr * 128:(pr + 1) * 128],
                             qF[0:64, pr * 128:(pr + 1) * 128],
                             start=True, stop=True)
                    return t.matmul(ps_b[:, 0:128],
                                    kF[64:128, pr * 128:(pr + 1) * 128],
                                    qF[64:128, pr * 128:(pr + 1) * 128],
                                    start=True, stop=True)
                p.t(f)
            p.on_t(c_A)
            tok_A = p.cnt["t"]

            def c_am(v, tok_A=tok_A):
                p.wait(v, "t", tok_A)
                p.wait(v, B["tok_ctab"][0])
                p.v(v, lambda: v.tensor_tensor(AMB[:, 0:128], ps_a[:, 0:128],
                                               TRIB, ALU.mult))
                p.v(v, lambda: v.tensor_tensor(AMB[:, 128:256], ps_b[:, 0:128],
                                               TRIB, ALU.mult))
            p.on_v(c_am)
            tok_am = p.cnt["v"]

            def c_nd(t, pr=pr, tok_am=tok_am):
                def f():
                    p.wait(t, "v", max(tok_am, p.cnt["v"]))
                    nb = 1024
                    t.matmul(ps_o[:, nb:nb + 512], identb, ZB,
                             start=True, stop=False)
                    t.matmul(ps_o[0:64, nb:nb + 128],
                             vc[:, (2 * pr) * 64:(2 * pr + 1) * 64],
                             AMB[:, 0:128], start=False, stop=False)
                    t.matmul(ps_o[64:128, nb:nb + 128],
                             vc[:, (2 * pr + 1) * 64:(2 * pr + 2) * 64],
                             AMB[:, 128:256], start=False, stop=False)
                    t.matmul(ps_o[0:64, nb:nb + 128], KVB[0:64, 0:64],
                             qF[0:64, pr * 128:(pr + 1) * 128],
                             start=False, stop=False)
                    t.matmul(ps_o[64:128, nb:nb + 128], KVB[64:128, 0:64],
                             qF[64:128, pr * 128:(pr + 1) * 128],
                             start=False, stop=False)
                    t.matmul(ps_o[0:1, nb + 128:nb + 256], onescb,
                             AMB[:, 0:128], start=False, stop=False)
                    t.matmul(ps_o[64:65, nb + 128:nb + 256], onescb,
                             AMB[:, 128:256], start=False, stop=False)
                    t.matmul(ps_o[0:1, nb + 128:nb + 256], KVB[0:64, 64:65],
                             qF[0:64, pr * 128:(pr + 1) * 128],
                             start=False, stop=False)
                    t.matmul(ps_o[64:65, nb + 128:nb + 256],
                             KVB[64:128, 64:65],
                             qF[64:128, pr * 128:(pr + 1) * 128],
                             start=False, stop=False)
                    return t.matmul(ps_o[:, nb:nb + 512], identb, ZB,
                                    start=False, stop=True)
                p.t(f)
            p.on_t(c_nd)
            tok_nd = p.cnt["t"]

            def c_den2(v, tok_nd=tok_nd):
                p.wait(v, "t", tok_nd)
                nb = 1024
                drow0 = ps_o[0:1, nb + 128:nb + 256]
                drow1 = ps_o[64:65, nb + 128:nb + 256]
                f0 = B["DENF"][0:1, 0:128]
                f1 = B["DENF"][0:1, 128:256]
                p.v(v, lambda: v.tensor_scalar_max(f0, drow0, 1e-6))
                p.v(v, lambda: v.tensor_scalar_max(f1, drow1, 1e-6))
                p.v(v, lambda: v.reciprocal(f0, f0))
                p.v(v, lambda: v.reciprocal(f1, f1))
                p.v(v, lambda: v.tensor_copy(RDB[0:1, 0:128], f0))
                p.v(v, lambda: v.tensor_copy(RDB[64:65, 0:128], f1))
            p.on_v(c_den2)
            tok_den = p.cnt["v"]

            def c_dbc(t, tok_den=tok_den):
                def f():
                    p.wait(t, "v", max(tok_den, p.cnt["v"]))
                    p.wait(t, B["tok_ctab"][0])
                    return t.matmul(ps_a[:, 0:128], PSL, RDB,
                                    start=True, stop=True)
                p.t(f)
            p.on_t(c_dbc)
            tok_dbc = p.cnt["t"]

            def c_dbce(v, tok_dbc=tok_dbc):
                p.wait(v, "t", tok_dbc)
                p.v(v, lambda: v.tensor_copy(DBC, ps_a[:, 0:128]))
            p.on_v(c_dbce)

            def c_y(v, pr=pr):
                nb = 1024
                p.v(v, lambda: v.tensor_tensor(
                    Yc[:, pr * 128:(pr + 1) * 128],
                    ps_o[:, nb:nb + 128], DBC, ALU.mult))
            p.on_v(c_y)

        state_update_group(ch, "B")

        for dh in range(2):
            def c_wo(t, dh=dh):
                def f():
                    p.wait(t, tok_wo)
                    p.wait(t, "v", p.cnt["v"])
                    t.matmul(ps_o[:, 1536:2048], identb, ZB,
                             start=True, stop=False)
                    for kt in range(8):
                        wsrc = wo_lo if kt < 6 else wo_hi
                        wco = kt * 1024 if kt < 6 else (kt - 6) * 1024
                        for dt in range(4):
                            d = dh * 4 + dt
                            t.matmul(
                                ps_o[:, 1536 + dt * 128:1536 + (dt + 1) * 128],
                                wsrc[:, wco + d * 128:wco + (d + 1) * 128],
                                Yc[:, kt * 128:(kt + 1) * 128],
                                start=False, stop=False)
                    return t.matmul(ps_o[:, 1536:2048], identb, ZB,
                                    start=False, stop=True)
                p.t(f)
            p.on_t(c_wo)
            tok_t = p.cnt["t"]

            def c_res(v, dh=dh, lo=lo, tok_t=tok_t):
                p.wait(v, "t", tok_t)
                for dt in range(4):
                    d = dh * 4 + dt
                    sl = slice(d * NTOK + lo, d * NTOK + lo + CH)
                    p.v(v, lambda sl=sl, dt=dt: v.tensor_add(
                        X[:, sl], X[:, sl],
                        ps_o[:, 1536 + dt * 128:1536 + (dt + 1) * 128]))
            p.on_v(c_res)


def alloc_common(nc, st):
    sb = lambda n, s, d: st.enter_context(nc.sbuf_tensor(n, s, d))[:, :]
    B = {}
    B["X"] = sb("Xr", [128, NDT * NTOK], F32)
    B["XN"] = sb("XNr", [128, NDT * NTOK], BF16)
    B["PA"] = sb("PA", [128, 14336], BF16)
    B["PB"] = sb("PB", [128, 16384], BF16)
    B["PC"] = sb("PC", [128, 7168], BF16)
    B["FT"] = sb("FT", [128, 2048], F32)
    B["RT"] = sb("RT", [128, 64], F32)
    B["ROWF"] = sb("ROWF", [1, 2048], F32)
    B["SC_EM"] = B["FT"][0:8, 0:2048]
    B["WRBF"] = sb("WRBF", [8, 2048], BF16)
    B["WVF"] = sb("WVF", [128, 128], F32)
    B["ident"] = sb("identb", [128, 128], BF16)
    B["identf"] = sb("identf_sb", [128, 128], F32)
    B["onesc"] = sb("onescb", [128, 1], BF16)
    B["onesrf"] = sb("onesrf_sb", [1, 128], F32)
    B["onesr"] = sb("onesrb", [1, 128], BF16)
    B["iota8"] = sb("iota8f", [128, 8], F32)
    B["ESEL"] = sb("ESEL", [8, 1024], BF16)
    B["WG"] = sb("WG", [128, 64], BF16)
    B["EPS"] = sb("EPSc", [1, 1], F32)
    B["CORRTM"] = sb("CORRTM", [128, 8], F32)
    B["XTRA"] = sb("XTRA", [128, 4096], BF16)
    B["BKV"] = B["XTRA"][:, 0:2048]
    B["BQO"] = sb("BQO", [128, 16], F32)
    B["KVMASK"] = sb("KVMASK", [128, 4], F32)
    B["DENF"] = sb("DENF", [1, 256], F32)
    B["STV"] = sb("STV", [128, 1024], F32)
    B["tok_x"] = [None]
    B["tok_consts"] = [None]
    B["tok_corr"] = [None]
    B["tok_bkv"] = [None]
    B["tok_ctab"] = [None]
    return B


def alloc_psum(nc, st):
    psd = lambda n, s, d: st.enter_context(nc.psum_tensor(n, s, d))
    PS = {}
    PS["a"] = psd("psa", [128, 512], F32)
    PS["b"] = psd("psb", [128, 512], F32)
    PS["c"] = psd("psc", [128, 512], F32)
    PS["d"] = psd("psd", [128, 512], F32)
    PS["o4"] = psd("pso", [128, 2048], F32)
    PS["trp"] = PS["d"][:, :][:, 0:64].bitcast(BF16)
    PS["r1"] = PS["a"][0:1, :]
    PS["r2"] = PS["b"][0:1, :]
    PS["r8"] = PS["b"][0:8, :]
    PS["bc"] = PS["a"][:, :]
    return PS


def declare_common_inputs(nc):
    W = {}
    W["consts"] = nc.dram_tensor("consts", [128, 16], F32, kind="ExternalInput").ap()
    W["ident"] = nc.dram_tensor("identf", [128, 128], F32, kind="ExternalInput").ap()
    W["onesrow"] = nc.dram_tensor("onesrow", [1, 128], F32, kind="ExternalInput").ap()
    W["esel"] = nc.dram_tensor("esel", [8, 1024], F32, kind="ExternalInput").ap()
    return W


def emit_prelude(p, B, W, xin):
    def c_init(g):
        ctx = p.nc.allow_non_contiguous_dma(reason="tiny const loads")
        ctx.__enter__()
        B["tok_x"][0] = p.dma(
            g, "d0",
            B["X"][:, :].rearrange("p (dt n) -> p dt n", n=NTOK),
            xin.rearrange("(dt p) n -> p dt n", p=128))
        p.dma(g, "d1", B["onesc"], W["consts"][:, 0:1])
        p.dma(g, "d1", B["iota8"], W["consts"][:, 2:10])
        p.dma(g, "d1", B["ident"], W["ident"])
        p.dma(g, "d1", B["identf"], W["ident"])
        p.dma(g, "d1", B["onesr"], W["onesrow"])
        p.dma(g, "d1", B["onesrf"], W["onesrow"])
        p.dma(g, "d1", B["EPS"], W["consts"][0:1, 10:11])
        B["tok_consts"][0] = p.dma(g, "d1", B["ESEL"], W["esel"])
        ctx.__exit__(None, None, None)
    p.on_g(c_init)


def emit_store(p, xout, B):
    def c_store(q):
        p.wait(q, "v", p.cnt["v"])
        p.wait(q, "s", p.cnt["s"])
        tok = p.dma(q, "d5", xout.rearrange("(dt p) n -> p dt n", p=128),
                    B["X"][:, :].rearrange("p (dt n) -> p dt n", n=NTOK))
        q.wait_ge(p.sem[tok[0]], tok[1])
    p.on_q(c_store)


def build_l0_ffn_program():
    nc = bass.Bass("TRN2", target_bir_lowering=False, debug=False)
    st = ExitStack()
    p = Prog(nc, st)
    W = declare_common_inputs(nc)
    xin = nc.dram_tensor("x_in", [D, NTOK], F32, kind="ExternalInput").ap()
    xout = nc.dram_tensor("x_out", [D, NTOK], F32, kind="ExternalOutput").ap()
    W["l0gate"] = nc.dram_tensor("l0gate", [D, 8], F32, kind="ExternalInput").ap()
    W["l0w1"] = nc.dram_tensor("l0w1", [8, D, 2048], F32, kind="ExternalInput").ap()
    W["l0w3"] = nc.dram_tensor("l0w3", [8, D, 2048], F32, kind="ExternalInput").ap()
    W["l0w2"] = nc.dram_tensor("l0w2", [8, 2048, D], F32, kind="ExternalInput").ap()

    B = alloc_common(nc, st)
    PS = alloc_psum(nc, st)
    with nc.Block() as blk:
        p.blk = blk
        emit_prelude(p, B, W, xin)
        emit_l0_ffn(p, B, W, PS)
        emit_store(p, xout, B)
    st.close()
    return nc


def build_l1_ffn_program():
    nc = bass.Bass("TRN2", target_bir_lowering=False, debug=False)
    st = ExitStack()
    p = Prog(nc, st)
    W = declare_common_inputs(nc)
    xin = nc.dram_tensor("x_in", [D, NTOK], F32, kind="ExternalInput").ap()
    xout = nc.dram_tensor("x_out", [D, NTOK], F32, kind="ExternalOutput").ap()
    W["l1gate"] = nc.dram_tensor("l1gate", [D, 8], F32, kind="ExternalInput").ap()
    W["l1corrtm"] = nc.dram_tensor("l1corrtm", [128, 8], F32, kind="ExternalInput").ap()
    W["l1w1"] = nc.dram_tensor("l1w1", [8, 512, 512], F32, kind="ExternalInput").ap()
    W["l1w2"] = nc.dram_tensor("l1w2", [8, 512, 512], F32, kind="ExternalInput").ap()
    W["l1fc1"] = nc.dram_tensor("l1fc1", [D, 512], F32, kind="ExternalInput").ap()
    W["l1fc2"] = nc.dram_tensor("l1fc2", [512, D], F32, kind="ExternalInput").ap()
    W["l1up"] = nc.dram_tensor("l1up", [D, 4096], F32, kind="ExternalInput").ap()
    W["l1dn"] = nc.dram_tensor("l1dn", [4096, D], F32, kind="ExternalInput").ap()

    B = alloc_common(nc, st)
    PS = alloc_psum(nc, st)
    with nc.Block() as blk:
        p.blk = blk
        emit_prelude(p, B, W, xin)

        def c_corr(g):
            p.dgroup(g, "d1")
            B["tok_corr"][0] = p.dma(g, "d1", B["CORRTM"], W["l1corrtm"])
        p.on_g(c_corr)

        emit_l1_ffn(p, B, W, PS)
        emit_store(p, xout, B)
    st.close()
    return nc


def build_l1_attn_program(groups):
    nc = bass.Bass("TRN2", target_bir_lowering=False, debug=False)
    st = ExitStack()
    p = Prog(nc, st)
    W = declare_common_inputs(nc)
    xin = nc.dram_tensor("x_in", [D, NTOK], F32, kind="ExternalInput").ap()
    xout = nc.dram_tensor("x_out", [D, NTOK], F32, kind="ExternalOutput").ap()
    for nm, shp in (("l1wq", [D, D]), ("l1wk", [D, D]), ("l1wv", [D, D]),
                    ("l1wo", [D, D]), ("l1bkv", [128, 2048]), ("l1bqo", [128, 16])):
        W[nm] = nc.dram_tensor(nm, shp, F32, kind="ExternalInput").ap()

    B = alloc_common(nc, st)
    PS = alloc_psum(nc, st)
    with nc.Block() as blk:
        p.blk = blk
        emit_prelude(p, B, W, xin)
        emit_l1_attn(p, B, W, PS, groups)
        emit_store(p, xout, B)
    st.close()
    return nc


def build_l0_attn_program(groups):
    nc = bass.Bass("TRN2", target_bir_lowering=False, debug=False)
    st = ExitStack()
    p = Prog(nc, st)
    W = declare_common_inputs(nc)
    xin = nc.dram_tensor("x_in", [D, NTOK], F32, kind="ExternalInput").ap()
    xout = nc.dram_tensor("x_out", [D, NTOK], F32, kind="ExternalOutput").ap()
    for nm, shp in (("l0wq", [D, D]), ("l0wk", [D, D]), ("l0wv", [D, D]),
                    ("l0wo", [D, D]), ("l0ctab", [128, 512]), ("l0stab", [128, 512]),
                    ("l0tri", [128, 128]), ("l0psel", [128, 128]),
                    ("l0kvmask", [128, 4])):
        W[nm] = nc.dram_tensor(nm, shp, F32, kind="ExternalInput").ap()

    B = alloc_common(nc, st)
    PS = alloc_psum(nc, st)
    with nc.Block() as blk:
        p.blk = blk
        emit_prelude(p, B, W, xin)
        emit_l0_attn(p, B, W, PS, groups)
        emit_store(p, xout, B)
    st.close()
    return nc


def _consts_array():
    c = np.zeros((128, 16), np.float32)
    c[:, 0] = 1.0
    c[:, 2:10] = np.tile(np.arange(8, dtype=np.float32), (128, 1))
    c[0, 10] = 1e-5
    return c


def _prep_weights(inputs):
    w = {}
    w["l0gate"] = np.ascontiguousarray(np.asarray(inputs["L0_ffn_gate_w"], np.float32).T)
    w["l0w1"] = np.ascontiguousarray(np.transpose(np.asarray(inputs["L0_ffn_w1"], np.float32), (0, 2, 1)))
    w["l0w3"] = np.ascontiguousarray(np.transpose(np.asarray(inputs["L0_ffn_w3"], np.float32), (0, 2, 1)))
    w["l0w2"] = np.ascontiguousarray(np.transpose(np.asarray(inputs["L0_ffn_w2"], np.float32), (0, 2, 1)))
    w["consts"] = _consts_array()
    w["identf"] = np.eye(128, dtype=np.float32)
    w["onesrow"] = np.ones((1, 128), dtype=np.float32)
    es = np.zeros((8, 1024), np.float32)
    for e in range(8):
        es[e, e * 128:(e + 1) * 128] = 1.0
    w["esel"] = es
    w["l1gate"] = np.ascontiguousarray(np.asarray(inputs["L1_ffn_gate"], np.float32).T)
    w["l1corrtm"] = np.tile(np.asarray(inputs["L1_ffn_corr_bias"], np.float32)[None, :], (128, 1))
    w["l1w1"] = np.ascontiguousarray(np.transpose(np.asarray(inputs["L1_ffn_w1"], np.float32), (0, 2, 1)))
    w["l1w2"] = np.ascontiguousarray(np.transpose(np.asarray(inputs["L1_ffn_w2"], np.float32), (0, 2, 1)))
    w["l1fc1"] = np.ascontiguousarray(np.asarray(inputs["L1_ffn_fc1_w"], np.float32).T)
    w["l1fc2"] = np.ascontiguousarray(np.asarray(inputs["L1_ffn_fc2_w"], np.float32).T)
    w["l1up"] = np.ascontiguousarray(np.asarray(inputs["L1_ffn_up_w"], np.float32).T)
    w["l1dn"] = np.ascontiguousarray(np.asarray(inputs["L1_ffn_down_w"], np.float32).T)
    wqkv = np.asarray(inputs["L1_attn_w_qkv"], np.float32)
    wp = wqkv.reshape(3, 128, 8, 1024).transpose(0, 2, 1, 3).reshape(3, 1024, 1024)
    w["l1wq"] = np.ascontiguousarray(wp[0].T)
    w["l1wk"] = np.ascontiguousarray(wp[1].T)
    w["l1wv"] = np.ascontiguousarray(wp[2].T)
    w["l1wo"] = np.ascontiguousarray(np.asarray(inputs["L1_attn_w_o"], np.float32).T)
    bp = np.asarray(inputs["L1_attn_b_qkv"], np.float32).reshape(3, 128, 8).transpose(0, 2, 1).reshape(3, 1024)
    bkv = np.zeros((128, 2048), np.float32)
    bkv[:, 0:1024] = bp[1][None, :]
    bkv[:, 1024:2048] = bp[2][None, :]
    w["l1bkv"] = bkv
    bqo = np.zeros((128, 16), np.float32)
    bqo[:, 0:8] = bp[0].reshape(8, 128).T
    bqo[:, 8:16] = np.asarray(inputs["L1_attn_b_o"], np.float32).reshape(8, 128).T
    w["l1bqo"] = bqo
    w["l0wq"] = np.ascontiguousarray(np.asarray(inputs["L0_attn_Wq"], np.float32).T)
    w["l0wk"] = np.ascontiguousarray(np.asarray(inputs["L0_attn_Wk"], np.float32).T)
    w["l0wv"] = np.ascontiguousarray(np.asarray(inputs["L0_attn_Wv"], np.float32).T)
    w["l0wo"] = np.ascontiguousarray(np.asarray(inputs["L0_attn_Wo"], np.float32).T)
    tri = np.tril(np.ones((128, 128), np.float32)).T
    w["l0tri"] = np.ascontiguousarray(tri)
    psl = np.zeros((128, 128), np.float32)
    psl[0, 0:64] = 1.0
    psl[64, 64:128] = 1.0
    w["l0psel"] = psl
    return w


def _core_tables(core):
    pos = np.arange((core % 4) * NTOK, (core % 4) * NTOK + NTOK, dtype=np.float32)
    inv = (np.float32(10000.0) ** (-np.arange(32, dtype=np.float32) / np.float32(32.0))).astype(np.float32)
    ang = (pos[:, None] * inv[None, :]).astype(np.float32)
    ct = np.cos(ang.astype(np.float64)).astype(np.float32).reshape(16, 128, 32).transpose(1, 0, 2).reshape(128, 512)
    st = np.sin(ang.astype(np.float64)).astype(np.float32).reshape(16, 128, 32).transpose(1, 0, 2).reshape(128, 512)
    m = np.zeros((128, 4), np.float32)
    m[:, 0:core % 4] = 1.0
    return {"l0ctab": np.ascontiguousarray(ct), "l0stab": np.ascontiguousarray(st),
            "l0kvmask": m}


def declare_all_weights(nc):
    W = declare_common_inputs(nc)
    W["l0gate"] = nc.dram_tensor("l0gate", [D, 8], F32, kind="ExternalInput").ap()
    W["l0w1"] = nc.dram_tensor("l0w1", [8, D, 2048], F32, kind="ExternalInput").ap()
    W["l0w3"] = nc.dram_tensor("l0w3", [8, D, 2048], F32, kind="ExternalInput").ap()
    W["l0w2"] = nc.dram_tensor("l0w2", [8, 2048, D], F32, kind="ExternalInput").ap()
    for nm, shp in (("l0wq", [D, D]), ("l0wk", [D, D]), ("l0wv", [D, D]),
                    ("l0wo", [D, D]), ("l0ctab", [128, 512]), ("l0stab", [128, 512]),
                    ("l0tri", [128, 128]), ("l0psel", [128, 128]),
                    ("l0kvmask", [128, 4])):
        W[nm] = nc.dram_tensor(nm, shp, F32, kind="ExternalInput").ap()
    W["l1gate"] = nc.dram_tensor("l1gate", [D, 8], F32, kind="ExternalInput").ap()
    W["l1corrtm"] = nc.dram_tensor("l1corrtm", [128, 8], F32, kind="ExternalInput").ap()
    W["l1w1"] = nc.dram_tensor("l1w1", [8, 512, 512], F32, kind="ExternalInput").ap()
    W["l1w2"] = nc.dram_tensor("l1w2", [8, 512, 512], F32, kind="ExternalInput").ap()
    W["l1fc1"] = nc.dram_tensor("l1fc1", [D, 512], F32, kind="ExternalInput").ap()
    W["l1fc2"] = nc.dram_tensor("l1fc2", [512, D], F32, kind="ExternalInput").ap()
    W["l1up"] = nc.dram_tensor("l1up", [D, 4096], F32, kind="ExternalInput").ap()
    W["l1dn"] = nc.dram_tensor("l1dn", [4096, D], F32, kind="ExternalInput").ap()
    for nm, shp in (("l1wq", [D, D]), ("l1wk", [D, D]), ("l1wv", [D, D]),
                    ("l1wo", [D, D]), ("l1bkv", [128, 2048]), ("l1bqo", [128, 16])):
        W[nm] = nc.dram_tensor(nm, shp, F32, kind="ExternalInput").ap()
    return W


def build_fused_program(groups):
    nc = bass.Bass("TRN2", target_bir_lowering=False, debug=False)
    st = ExitStack()
    p = Prog(nc, st)
    W = declare_all_weights(nc)
    xin = nc.dram_tensor("x_in", [D, NTOK], F32, kind="ExternalInput").ap()
    xout = nc.dram_tensor("x_out", [D, NTOK], F32, kind="ExternalOutput").ap()
    B = alloc_common(nc, st)
    PS = alloc_psum(nc, st)
    with nc.Block() as blk:
        p.blk = blk
        emit_prelude(p, B, W, xin)

        def c_corr(g):
            p.dgroup(g, "d1")
            B["tok_corr"][0] = p.dma(g, "d1", B["CORRTM"], W["l1corrtm"])
        p.on_g(c_corr)

        emit_l0_ffn(p, B, W, PS)
        emit_l0_attn(p, B, W, PS, groups)
        emit_l0_ffn(p, B, W, PS)
        emit_l1_ffn(p, B, W, PS)
        emit_l1_attn(p, B, W, PS, groups)
        emit_l1_ffn(p, B, W, PS)
        emit_store(p, xout, B)
    st.close()
    return nc


def _input_names(nc):
    names = set()
    for f in nc.m.functions:
        for alloc in f.allocations:
            if getattr(alloc, "kind", None) == "ExternalInput":
                for m in alloc.memorylocations or []:
                    names.add(m.name)
    return names


def kernel(**inputs):
    x = np.asarray(inputs["L0_attn_x"], np.float32).reshape(2 * 8192, D)
    shards = [np.ascontiguousarray(x[c * NTOK:(c + 1) * NTOK].T) for c in range(NCORE)]
    w = _prep_weights(inputs)
    tables = [_core_tables(c) for c in range(NCORE)]
    groups = [[0, 1, 2, 3], [4, 5, 6, 7]]

    nc = build_fused_program(groups)
    names = _input_names(nc)
    in_maps = []
    for c in range(NCORE):
        full = {"x_in": shards[c], **w, **tables[c]}
        in_maps.append({k: v for k, v in full.items() if k in names})
    res = run_bass_kernel_spmd(nc, in_maps, core_ids=list(range(NCORE)))
    cur = [res.results[c]["x_out"] for c in range(NCORE)]
    out = np.concatenate([cur[c].T for c in range(NCORE)], axis=0)
    return np.ascontiguousarray(out.reshape(2, 8192, D))
